# version 02
# baseline (speedup 1.0000x reference)
import numpy as np
from contextlib import ExitStack
import concourse.bass as bass
import concourse.mybir as mybir
from concourse.bass_utils import run_bass_kernel_spmd

F32 = mybir.dt.float32
BF16 = mybir.dt.bfloat16
AF = mybir.ActivationFunctionType
ALU = mybir.AluOpType
AX = mybir.AxisListType

_DT = {np.dtype(np.float32): F32, np.dtype(np.int32): mybir.dt.int32}


def np_to_dt(d):
    return _DT[np.dtype(d)]


class P:

    def __init__(self, name="prog"):
        self.nc = bass.Bass("TRN2", target_bir_lowering=False)
        self.stack = ExitStack()
        self.sem = self.stack.enter_context(self.nc.semaphore("gsem"))
        self.cnt = 0
        self.qs = ("sync", "tensor", "vector", "scalar", "gpsimd")
        self.ops = {q: [] for q in self.qs}
        self.io = {}
        self._grp = None
        self._scopes = []
        self._uid = 0

    def uname(self, base):
        self._uid += 1
        return f"{base}_{self._uid}"

    def scope(self):
        outer = self

        class _S:
            def __enter__(self):
                outer._scopes.append(ExitStack())

            def __exit__(self, *a):
                outer._scopes.pop().close()

        return _S()

    def dram(self, name, shape, dtype=F32, kind="Internal"):
        return self.nc.dram_tensor(name, list(shape), dtype, kind=kind).ap()

    def inp(self, name, arr):
        arr = np.ascontiguousarray(arr)
        self.io[name] = arr
        return self.dram(name, arr.shape, np_to_dt(arr.dtype), kind="ExternalInput")

    def out(self, name, shape, dtype=F32):
        return self.dram(name, shape, dtype, kind="ExternalOutput")

    def sb(self, name, shape, dtype=F32):
        st = self._scopes[-1] if self._scopes else self.stack
        return st.enter_context(self.nc.sbuf_tensor(self.uname(name), list(shape), dtype))

    def ps(self, name, shape, dtype=F32):
        st = self._scopes[-1] if self._scopes else self.stack
        return st.enter_context(self.nc.psum_tensor(self.uname(name), list(shape), dtype))

    def _emit(self, q, fn, inc):
        if self._grp is None:
            self.ops[q].append((self.cnt, fn, inc))
            self.cnt += inc
        else:
            self.ops[q].append((self._grp[0], fn, inc))
            self._grp[1] += inc

    def group(self):
        outer = self

        class _G:
            def __enter__(self):
                outer._grp = [outer.cnt, 0]

            def __exit__(self, *a):
                outer.cnt += outer._grp[1]
                outer._grp = None

        return _G()

    def dma(self, dst, src, q="sync", **kw):
        self._emit(q, lambda e, d=dst, s=src: e.dma_start(d, s, **kw), 16)

    def op(self, q, fn):
        self._emit(q, fn, 1)

    def mm(self, out, lhsT, rhs, start=True, stop=True):
        self._emit("tensor", lambda e: e.matmul(out, lhsT, rhs, start=start, stop=stop), 1)

    def trans(self, out, in_, ident):
        self._emit("tensor", lambda e: e.transpose(out, in_, ident), 1)

    def act(self, out, in_, func, bias=..., scale=...):
        kw = {}
        if bias is not ...:
            kw["bias"] = bias
        if scale is not ...:
            kw["scale"] = scale
        self._emit("scalar", lambda e: e.activation(out, in_, func, **kw), 1)

    def copy(self, out, in_, q="scalar"):
        if q == "scalar":
            self._emit(q, lambda e: e.copy(out, in_), 1)
        else:
            self._emit(q, lambda e: e.tensor_copy(out, in_), 1)

    def tt(self, out, in0, in1, opname, q="vector"):
        o = getattr(ALU, opname)
        self._emit(q, lambda e: e.tensor_tensor(out, in0, in1, o), 1)

    def ts(self, out, in0, scalar, opname, q="vector"):
        o = getattr(ALU, opname)
        self._emit(q, lambda e: e.tensor_scalar(out, in0, scalar, None, o), 1)

    def stt(self, out, in0, scalar, in1, op0, op1, q="vector"):
        o0, o1 = getattr(ALU, op0), getattr(ALU, op1)
        self._emit(q, lambda e: e.scalar_tensor_tensor(out, in0, scalar, in1, o0, o1), 1)

    def red(self, out, in_, opname, negate=None, q="vector"):
        o = getattr(ALU, opname)
        self._emit(q, lambda e: e.tensor_reduce(out, in_, AX.X, o, negate=negate), 1)

    def memset(self, ap, v, q="vector"):
        self._emit(q, lambda e: e.memset(ap, v), 1)

    def iota(self, ap, pattern, base=0, channel_multiplier=0):
        self._emit(
            "gpsimd",
            lambda e: e.iota(ap, pattern, base=base, channel_multiplier=channel_multiplier,
                             allow_small_or_imprecise_dtypes=True),
            1,
        )

    def finalize(self):
        nc = self.nc
        total = self.cnt
        sem = self.sem

        def runner(eng, lst, final=False):
            for (val, fn, inc) in lst:
                if val > 0:
                    eng.wait_ge(sem, val)
                inst = fn(eng)
                inst.then_inc(sem, inc)
            if final:
                eng.wait_ge(sem, total)

        with nc.Block() as block:
            if self.ops["sync"] or True:
                @block.sync
                def _(eng):
                    runner(eng, self.ops["sync"], final=True)
            if self.ops["tensor"]:
                @block.tensor
                def _(eng):
                    runner(eng, self.ops["tensor"])
            if self.ops["vector"]:
                @block.vector
                def _(eng):
                    runner(eng, self.ops["vector"])
            if self.ops["scalar"]:
                @block.scalar
                def _(eng):
                    runner(eng, self.ops["scalar"])
            if self.ops["gpsimd"]:
                @block.gpsimd
                def _(eng):
                    runner(eng, self.ops["gpsimd"])
        return nc

    def run(self, in_maps=None, n_cores=1, **kw):
        nc = self.finalize()
        if in_maps is None:
            in_maps = [dict(self.io) for _ in range(n_cores)]
        res = run_bass_kernel_spmd(nc, in_maps, core_ids=list(range(n_cores)), **kw)
        self.stack.close()
        return res



D = 1024
SEQ = 2048
NT = SEQ // 128


def host_consts():
    C = {}
    C["c_ident"] = np.eye(128, dtype=np.float32)
    p_ = np.arange(64)[:, None]
    f_ = np.arange(33)[None, :]
    ang = 2 * np.pi * p_ * f_ / 64.0
    dftc = np.concatenate([np.cos(ang), -np.sin(ang)], axis=1).astype(np.float32)
    C["c_dft"] = dftc
    t_ = np.arange(64)[None, :]
    f2 = np.arange(33)[:, None]
    w = np.where((f2 == 0) | (f2 == 32), 1.0, 2.0)
    ang2 = 2 * np.pi * f2 * t_ / 64.0
    idft = np.concatenate([w * np.cos(ang2) / 64.0, -w * np.sin(ang2) / 64.0], axis=0).astype(np.float32)
    C["c_idft"] = idft
    xr = np.random.RandomState(1).randn(5, 64).astype(np.float64)
    X = np.fft.rfft(xr, axis=1)
    Xm = xr @ dftc.astype(np.float64)
    assert np.allclose(Xm[:, :33], X.real, atol=1e-6) and np.allclose(Xm[:, 33:], X.imag, atol=1e-6)
    yb = np.concatenate([X.real, X.imag], 1) @ idft.astype(np.float64)
    assert np.allclose(yb, xr, atol=1e-9)
    i_ = np.arange(128)
    m = (i_[:, None] // 4 >= i_[None, :] // 4)
    C["c_bmask"] = np.where(m, 0.0, -1e30).astype(np.float32)
    m2 = (i_[:, None] >= i_[None, :])
    C["c_tmask"] = np.where(m2, 0.0, -1e30).astype(np.float32)
    return C


def prep_weights(inp):
    T = lambda a: np.ascontiguousarray(np.asarray(a, np.float32).T)
    A = lambda a: np.ascontiguousarray(np.asarray(a, np.float32))
    W = {}
    W["a0_qwT"] = T(inp["L0_attn_a_q_w"])
    W["a0_kwT"] = T(inp["L0_attn_a_k_w"])
    W["a0_vwT"] = T(inp["L0_attn_a_v_w"])
    W["a0_owT"] = T(inp["L0_attn_a_o_w"])
    W["a0_dwr"] = A(inp["L0_attn_a_dw_r"]).reshape(1024, 7)
    W["a0_dwi"] = A(inp["L0_attn_a_dw_i"]).reshape(1024, 7)
    W["a0_pwrT"] = A(inp["L0_attn_a_pw_r"])
    W["a0_pwrT"] = T(inp["L0_attn_a_pw_r"])
    W["a0_pwiT"] = T(inp["L0_attn_a_pw_i"])
    W["a0_gwT"] = T(inp["L0_attn_a_gate_w"])
    W["a0_alpha"] = A(inp["L0_attn_a_alpha_raw"]).reshape(1, 1)
    W["b0_logdt"] = A(inp["L0_attn_b_log_dt"]).reshape(1, 1024)
    W["b0_lwr"] = A(inp["L0_attn_b_log_w_real"]).reshape(1, 32)
    W["b0_wim"] = A(inp["L0_attn_b_w_imag"]).reshape(1, 32)
    W["b0_Cr"] = A(inp["L0_attn_b_C_real"]).reshape(1024, 32)
    W["b0_Ci"] = A(inp["L0_attn_b_C_imag"]).reshape(1024, 32)
    W["b0_D"] = A(inp["L0_attn_b_D"]).reshape(1, 1024)
    W["b0_WoT"] = T(inp["L0_attn_b_W_out"])
    W["b0_bo"] = A(inp["L0_attn_b_b_out"]).reshape(1, 1024)
    W["f0_sinit"] = A(inp["L0_ffn_slot_init"])
    W["f0_sscale"] = A(inp["L0_ffn_slot_scale"])
    w1 = A(inp["L0_ffn_w1"])
    W["f0_w1aT"] = T(w1[:, :1024])
    W["f0_w1bT"] = T(w1[:, 1024:])
    W["f0_b1"] = A(inp["L0_ffn_b1"]).reshape(1, 512)
    W["f0_w2T"] = T(inp["L0_ffn_w2"])
    W["f0_b2"] = A(inp["L0_ffn_b2"]).reshape(1, 64)
    W["f0_wihT"] = T(inp["L0_ffn_gru_wih"])
    W["f0_whhT"] = T(inp["L0_ffn_gru_whh"])
    W["f0_bih"] = A(inp["L0_ffn_gru_bih"]).reshape(1, 3072)
    W["f0_bhh"] = A(inp["L0_ffn_gru_bhh"]).reshape(1, 3072)
    W["f0_wvT"] = T(inp["L0_ffn_wv"])
    W["f0_bv"] = A(inp["L0_ffn_bv"]).reshape(1, 1024)
    W["f0_woT"] = T(inp["L0_ffn_wo"])
    W["f0_bo"] = A(inp["L0_ffn_bo"]).reshape(1, 1024)
    W["f0_tau"] = A(inp["L0_ffn_tau"]).reshape(1, 1)
    W["a1_nw"] = A(inp["L1_attn_a_norm_w"]).reshape(1, 1024)
    W["a1_Wq"] = A(inp["L1_attn_a_Wq"])
    W["a1_Wk"] = A(inp["L1_attn_a_Wk"])
    W["a1_Wv"] = A(inp["L1_attn_a_Wv"])
    W["a1_Wo"] = A(inp["L1_attn_a_Wo"])
    W["a1_gate"] = A(inp["L1_attn_a_gate"]).reshape(1, 1)
    W["a1_ltemp"] = A(inp["L1_attn_a_log_temperature"]).reshape(1, 1)
    W["b1_Wq"] = A(inp["L1_attn_b_Wq"])
    W["b1_bq"] = A(inp["L1_attn_b_bq"]).reshape(1024)
    W["b1_Wl"] = A(inp["L1_attn_b_Wl"])
    W["b1_bl"] = A(inp["L1_attn_b_bl"]).reshape(256)
    W["b1_Wk"] = A(inp["L1_attn_b_Wk"])
    W["b1_bk"] = A(inp["L1_attn_b_bk"]).reshape(1024)
    W["b1_Wv"] = A(inp["L1_attn_b_Wv"])
    W["b1_bv"] = A(inp["L1_attn_b_bv"]).reshape(1, 1024)
    W["b1_Wo"] = A(inp["L1_attn_b_Wo"])
    W["b1_bo"] = A(inp["L1_attn_b_bo"]).reshape(1, 1024)
    W["b1_bq_pg"] = A(W["b1_bq"]).reshape(8, 128).T.copy()
    W["b1_bl_pg"] = A(W["b1_bl"]).reshape(2, 128).T.copy()
    W["b1_bk_pg"] = A(W["b1_bk"]).reshape(8, 128).T.copy()
    W["b1_Wo_hs"] = np.ascontiguousarray(A(inp["L1_attn_b_Wo"]).reshape(16, 64, 1024).transpose(1, 0, 2))
    W["a1_nw_pg"] = A(inp["L1_attn_a_norm_w"]).reshape(8, 128).T.copy()
    W["b0_logdt_pg"] = A(inp["L0_attn_b_log_dt"]).reshape(8, 128).T.copy()
    W["b0_D_pg"] = A(inp["L0_attn_b_D"]).reshape(8, 128).T.copy()
    W["f1_rwT"] = T(inp["L1_ffn_router_w"])
    W["f1_w1T"] = np.ascontiguousarray(np.asarray(inp["L1_ffn_w1"], np.float32).transpose(0, 2, 1))
    W["f1_w2T"] = np.ascontiguousarray(np.asarray(inp["L1_ffn_w2"], np.float32).transpose(0, 2, 1))
    W.update(host_consts())
    return {k: np.ascontiguousarray(v, dtype=np.float32) for k, v in W.items()}


class Ctx:
    def __init__(self, p, B):
        self.B = B
        self.ident = p.sb("ident", (128, 128), F32)
        self.identb = p.sb("identb", (128, 128), BF16)
        self.ones1b = p.sb("ones1b", (1, 128), BF16)
        self.onescolb = p.sb("onescolb", (128, 1), BF16)
        self.pst_b = p.ps("pst_b", (128, 128), BF16)
        self.pst_f = p.ps("pst_f", (128, 128), F32)
        self.t0 = p.sb("ct0", (128, 1024), F32)
        self.t1 = p.sb("ct1", (128, 1024), F32)
        self.c0 = p.sb("cc0", (128, 1), F32)
        self.c1 = p.sb("cc1", (128, 1), F32)
        self.bcast = p.dram("cbcast", (1, 16), F32)


def ctx_init(p, c, Wd):
    p.dma(c.ident[:, :], Wd["c_ident"][:, :])
    p.copy(c.identb[:, :], c.ident[:, :], q="vector")
    p.memset(c.ones1b[:, :], 1.0)
    p.memset(c.onescolb[:, :], 1.0)


def ln_tile(p, c, dst, src, eps=1e-5):
    p.red(c.c0[:, :], src, "add")
    p.ts(c.c0[:, :], c.c0[:, :], 1.0 / D, "mult")
    p.ts(dst, src, c.c0[:, :], "subtract")
    p.act(c.t1[:, :], dst, AF.Square)
    p.red(c.c1[:, :], c.t1[:, :], "add")
    p.ts(c.c1[:, :], c.c1[:, :], 1.0 / D, "mult")
    p.ts(c.c1[:, :], c.c1[:, :], eps, "add")
    p.act(c.c1[:, :], c.c1[:, :], AF.Ln)
    p.ts(c.c1[:, :], c.c1[:, :], -0.5, "mult")
    p.act(c.c1[:, :], c.c1[:, :], AF.Exp)
    p.ts(dst, dst, c.c1[:, :], "mult")


def tr128(p, c, dst, src, r, k, fp32=False):
    ps = c.pst_f if fp32 else c.pst_b
    idn = c.ident if fp32 else c.identb
    p.trans(ps[0:k, 0:r], src, idn[0:r, 0:r])
    p.copy(dst, ps[0:k, 0:r], q="vector")


def loadw_bf(p, c, name, dram, kt, n):
    b = p.sb(name, (128, kt, n), BF16)
    for k in range(kt):
        for c0 in range(0, n, 1024):
            cw = min(1024, n - c0)
            p.dma(c.t0[:, 0:cw], dram[k * 128:(k + 1) * 128, c0:c0 + cw])
            p.copy(b[:, k, c0:c0 + cw], c.t0[:, 0:cw], q="vector")
    return b


def loadrow_bf(p, name, dram, n):
    f = p.sb(name + "f", (1, n), F32)
    b = p.sb(name, (1, n), BF16)
    p.dma(f[:, :], dram[:, :])
    p.copy(b[:, :], f[:, :], q="vector")
    return b, f


def scalar_col(p, c, name, dram11, fn=None, slot=0):
    s = p.sb(name + "s", (1, 1), F32)
    col = p.sb(name, (128, 1), F32)
    p.dma(s[:, :], dram11[:, :])
    if fn is not None:
        fn(s)
    p.dma(c.bcast[0:1, slot:slot + 1], s[:, :])
    p.dma(col[:, :], c.bcast[0:1, slot:slot + 1].broadcast_to([128, 1]))
    return col


def lnT_tiles(p, c, xin_b, lnT, lnb_all=None, rms=False, eps=1e-5):
    for t in range(NT):
        p.dma(c.t0[:, :], xin_b[t * 128:(t + 1) * 128, :])
        ln_tile(p, c, c.t0[:, :], c.t0[:, :], eps=eps)
        lb = p.sb("lnb", (128, 1024), BF16) if lnb_all is None else None
        tgt = lnb_all[:, t, :] if lnb_all is not None else lb[:, :]
        p.copy(tgt, c.t0[:, :], q="vector")
        for g in range(8):
            tr128(p, c, lnT[:, g, t * 128:(t + 1) * 128], tgt[:, g * 128:(g + 1) * 128] if lnb_all is not None else lb[:, g * 128:(g + 1) * 128], 128, 128)


def emit_l1ffn(p, c, xin, xout, Wd, tag):
    B = c.B
    ntiles = B * NT
    w1bf = p.dram(tag + "w1bf", (8, 1024, 4096), BF16)
    w2bf = p.dram(tag + "w2bf", (8, 4096, 1024), BF16)
    with p.scope():
        f = p.sb("wcast", (128, 4096), F32)
        b = p.sb("wcastb", (128, 4096), BF16)
        for e in range(8):
            for k in range(8):
                p.dma(f[:, :], Wd["f1_w1T"][e, k * 128:(k + 1) * 128, :])
                p.copy(b[:, :], f[:, :], q="vector")
                p.dma(w1bf[e, k * 128:(k + 1) * 128, :], b[:, :])
            for k in range(32):
                p.dma(f[:, 0:1024], Wd["f1_w2T"][e, k * 128:(k + 1) * 128, :])
                p.copy(b[:, 0:1024], f[:, 0:1024], q="vector")
                p.dma(w2bf[e, k * 128:(k + 1) * 128, :], b[:, 0:1024])

    xTd = p.dram(tag + "xT", (ntiles, 8, 128, 128), BF16)
    accd = p.dram(tag + "acc", (ntiles, 128, 1024), F32)
    wts = p.sb(tag + "wts", (128, ntiles, 8), F32)

    with p.scope():
        rwT = loadw_bf(p, c, "rwT", Wd["f1_rwT"], 8, 8)
        xT = p.sb("xTt", (128, 8, 128), BF16)
        lnb = p.sb("lnbt", (128, 1024), BF16)
        ps_r = p.ps("ps_r", (128, 8), F32)
        lg = p.sb("lg", (128, 8), F32)
        wk = p.sb("wkk", (128, 8), F32)
        kp = p.sb("kpp", (128, 8), F32)
        for bi in range(B):
            for t in range(NT):
                ti = bi * NT + t
                p.dma(c.t0[:, :], xin[bi, t * 128:(t + 1) * 128, :])
                ln_tile(p, c, c.t0[:, :], c.t0[:, :])
                p.copy(lnb[:, :], c.t0[:, :], q="vector")
                for g in range(8):
                    tr128(p, c, xT[:, g, :], lnb[:, g * 128:(g + 1) * 128], 128, 128)
                p.dma(xTd[ti].rearrange("k pp cc -> pp k cc"), xT[:, :, :])
                for k in range(8):
                    p.mm(ps_r[:, :], xT[:, k, :], rwT[:, k, :], start=(k == 0), stop=(k == 7))
                p.red(c.c0[:, :], ps_r[:, :], "max", negate=True)
                p.act(lg[:, :], ps_r[:, :], AF.Exp, bias=c.c0[:, :])
                p.red(c.c1[:, :], lg[:, :], "add")
                p.op("vector", lambda e: e.reciprocal(c.c1[:, :], c.c1[:, :]))
                p.ts(lg[:, :], lg[:, :], c.c1[:, :], "mult")
                p.copy(wk[:, :], lg[:, :], q="vector")
                p.red(c.c0[:, :], wk[:, :], "max")
                p.ts(kp[:, :], wk[:, :], c.c0[:, :], "is_ge")
                p.stt(wk[:, :], kp[:, :], -1e30, wk[:, :], "mult", "add")
                p.red(c.c0[:, :], wk[:, :], "max")
                p.ts(kp[:, :], lg[:, :], c.c0[:, :], "is_ge")
                p.tt(wts[:, ti, :], lg[:, :], kp[:, :], "mult")
        if getattr(p, "dbg_l1ffn", False):
            wo_ = p.out(tag + "_dbg_wts", (128, ntiles, 8), F32)
            p.dma(wo_, wts[:, :, :])

    if getattr(p, "dbg_l1ffn", False):
        xo_ = p.out(tag + "_dbg_xT", (ntiles, 8, 128, 128), BF16)
        with p.scope():
            tmpx = p.sb("tmpx", (128, 8, 128), BF16)
            for ti in range(ntiles):
                p.dma(tmpx[:, :, :], xTd[ti].rearrange("k pp cc -> pp k cc"))
                p.dma(xo_[ti].rearrange("k pp cc -> pp k cc"), tmpx[:, :, :])
    GRP = 4
    ngrp = ntiles // GRP
    with p.scope():
        w1h = p.sb("w1h", (128, 8, 4096), BF16)
        w2h = p.sb("w2h", (128, 32, 1024), BF16)
        xTh = p.sb("xTh", (128, 8, GRP * 128), BF16)
        hTb = p.sb("hTb", (128, 32, GRP * 128), BF16)
        at = p.sb("at", (128, 1024), F32)
        ps_h = p.ps("ps_h", (128, 512), F32)
        ps_y = p.ps("ps_y", (128, 512), F32)
        for e in range(8):
            for k in range(8):
                p.dma(w1h[:, k, :], w1bf[e, k * 128:(k + 1) * 128, :])
            for k in range(32):
                p.dma(w2h[:, k, :], w2bf[e, k * 128:(k + 1) * 128, :])
            for gi in range(ngrp):
                for i in range(GRP):
                    ti = gi * GRP + i
                    p.dma(xTh[:, :, i * 128:(i + 1) * 128], xTd[ti].rearrange("k pp cc -> pp k cc"))
                for ht in range(32):
                    for k in range(8):
                        p.mm(ps_h[:, :], w1h[:, k, ht * 128:(ht + 1) * 128], xTh[:, k, :], start=(k == 0), stop=(k == 7))
                    p.act(hTb[:, ht, :], ps_h[:, :], AF.Gelu)
                for i in range(GRP):
                    ti = gi * GRP + i
                    if e == 0:
                        p.memset(at[:, :], 0.0)
                    else:
                        p.dma(at[:, :], accd[ti])
                    for g2 in range(2):
                        for k in range(32):
                            p.mm(ps_y[:, :], hTb[:, k, i * 128:(i + 1) * 128], w2h[:, k, g2 * 512:(g2 + 1) * 512], start=(k == 0), stop=(k == 31))
                        p.stt(at[:, g2 * 512:(g2 + 1) * 512], ps_y[:, :], wts[:, ti, e:e + 1], at[:, g2 * 512:(g2 + 1) * 512], "mult", "add")
                    p.dma(accd[ti], at[:, :])
    dbg_acc = p.out(tag + "_dbg_acc", (ntiles, 128, 1024), F32) if getattr(p, "dbg_l1ffn", False) else None
    with p.scope():
        at = p.sb("resa", (128, 1024), F32)
        for bi in range(B):
            for t in range(NT):
                ti = bi * NT + t
                p.dma(at[:, :], accd[ti])
                if dbg_acc is not None:
                    p.dma(dbg_acc[ti], at[:, :])
                p.dma(c.t0[:, :], xin[bi, t * 128:(t + 1) * 128, :])
                p.tt(at[:, :], at[:, :], c.t0[:, :], "add")
                p.dma(xout[bi, t * 128:(t + 1) * 128, :], at[:, :])


def emit_l1attnb(p, c, xin, xout, Wd, tag):
    B = c.B
    qTd = p.dram(tag + "qTd", (8, 128, SEQ), BF16)
    kTd = p.dram(tag + "kTd", (8, 128, SEQ), BF16)
    vd = p.dram(tag + "vd", (NT, 128, 1024), BF16)
    accd = p.dram(tag + "accd", (NT, 128, 1024), F32)
    with p.scope():
        wq = loadw_bf(p, c, "wq", Wd["b1_Wq"], 8, 1024)
        wl = loadw_bf(p, c, "wl", Wd["b1_Wl"], 8, 256)
        wk = loadw_bf(p, c, "wk", Wd["b1_Wk"], 2, 1024)
        wv = loadw_bf(p, c, "wv", Wd["b1_Wv"], 2, 1024)
        wo = p.sb("wohs", (64, 16, 1024), BF16)
        wof = p.sb("wohsf", (64, 1024), F32)
        for h in range(16):
            p.dma(wof[:, :], Wd["b1_Wo_hs"][:, h, :])
            p.copy(wo[:, h, :], wof[:, :], q="vector")
        bq = p.sb("bqc", (128, 8), F32)
        bl = p.sb("blc", (128, 2), F32)
        bk = p.sb("bkc", (128, 8), F32)
        p.dma(bq[:, :], Wd["b1_bq_pg"][:, :])
        p.dma(bl[:, :], Wd["b1_bl_pg"][:, :])
        p.dma(bk[:, :], Wd["b1_bk_pg"][:, :])
        bvb, _ = loadrow_bf(p, "bvr", Wd["b1_bv"], 1024)
        bo32 = p.sb("bo32", (128, 1024), F32)
        p.dma(bo32[:, :], Wd["b1_bo"].broadcast_to([128, 1024]))
        ps1 = p.ps("ps1", (128, 512), F32)
        ps_cs = p.ps("ps_cs", (1, 512), F32)
        ps_pv = p.ps("ps_pv", (64, 512), F32)
        ps_o = p.ps("ps_o", (128, 512), F32)
        ev = p.sb("evac", (128, 512), BF16)

        for bi in range(B):
            with p.scope():
                hT = p.sb("hT", (128, 8, SEQ), BF16)
                latT = p.sb("latT", (128, 2, SEQ), BF16)
                lnT_tiles(p, c, xin[bi], hT)
                for g in range(2):
                    for g2 in range(4):
                        for k in range(8):
                            p.mm(ps1[:, :], wl[:, k, g * 128:(g + 1) * 128], hT[:, k, g2 * 512:(g2 + 1) * 512], start=(k == 0), stop=(k == 7))
                        p.act(latT[:, g, g2 * 512:(g2 + 1) * 512], ps1[:, :], AF.Identity, bias=bl[:, g:g + 1])
                for g in range(8):
                    for g2 in range(4):
                        for k in range(8):
                            p.mm(ps1[:, :], wq[:, k, g * 128:(g + 1) * 128], hT[:, k, g2 * 512:(g2 + 1) * 512], start=(k == 0), stop=(k == 7))
                        p.act(ev[:, :], ps1[:, :], AF.Identity, bias=bq[:, g:g + 1])
                        p.dma(qTd[g, :, g2 * 512:(g2 + 1) * 512], ev[:, :])
                        for k in range(2):
                            p.mm(ps1[:, :], wk[:, k, g * 128:(g + 1) * 128], latT[:, k, g2 * 512:(g2 + 1) * 512], start=(k == 0), stop=(k == 1))
                        p.act(ev[:, :], ps1[:, :], AF.Identity, bias=bk[:, g:g + 1])
                        p.dma(kTd[g, :, g2 * 512:(g2 + 1) * 512], ev[:, :])
                for t in range(NT):
                    for g2 in range(2):
                        for k in range(2):
                            p.mm(ps1[:, :], latT[:, k, t * 128:(t + 1) * 128], wv[:, k, g2 * 512:(g2 + 1) * 512], start=(k == 0), stop=False)
                        p.mm(ps1[:, :], c.ones1b[:, :], bvb[:, g2 * 512:(g2 + 1) * 512], start=False, stop=True)
                        p.copy(ev[:, :], ps1[:, :], q="vector")
                        p.dma(vd[t, :, g2 * 512:(g2 + 1) * 512], ev[:, :])
            with p.scope():
                qTg = p.sb("qTg", (64, SEQ), BF16)
                kTg = p.sb("kTg", (64, SEQ), BF16)
                vh = p.sb("vh", (128, NT, 64), BF16)
                eST = p.sb("eST", (128, NT, SEQ), BF16)
                sums = p.sb("sums", (1, SEQ), F32)
                rcp = p.sb("rcp", (128, NT), F32)
                attnTh = p.sb("attnTh", (64, SEQ), BF16)
                at = p.sb("at", (128, 1024), F32)
                for h in range(16):
                    g = h // 2
                    off = (h % 2) * 64
                    p.dma(qTg[:, :], qTd[g, off:off + 64, :])
                    p.dma(kTg[:, :], kTd[g, off:off + 64, :])
                    for t in range(NT):
                        p.dma(vh[:, t, :], vd[t, :, h * 64:(h + 1) * 64])
                    for kt in range(NT):
                        for g2 in range(4):
                            p.mm(ps1[:, :], kTg[:, kt * 128:(kt + 1) * 128], qTg[:, g2 * 512:(g2 + 1) * 512], start=True, stop=True)
                            p.act(eST[:, kt, g2 * 512:(g2 + 1) * 512], ps1[:, :], AF.Exp, scale=0.125)
                    for g2 in range(4):
                        for kt in range(NT):
                            p.mm(ps_cs[:, :], c.onescolb[:, :], eST[:, kt, g2 * 512:(g2 + 1) * 512], start=(kt == 0), stop=(kt == NT - 1))
                        p.copy(sums[:, g2 * 512:(g2 + 1) * 512], ps_cs[:, :])
                    for t in range(NT):
                        tr128(p, c, rcp[:, t:t + 1], sums[:, t * 128:(t + 1) * 128], 1, 128, fp32=True)
                    p.op("vector", lambda e: e.reciprocal(rcp[:, :], rcp[:, :]))
                    for g2 in range(4):
                        for kt in range(NT):
                            p.mm(ps_pv[:, :], vh[:, kt, :], eST[:, kt, g2 * 512:(g2 + 1) * 512], start=(kt == 0), stop=(kt == NT - 1))
                        p.copy(attnTh[:, g2 * 512:(g2 + 1) * 512], ps_pv[:, :], q="vector")
                    for t in range(NT):
                        if h == 0:
                            p.memset(at[:, :], 0.0)
                        else:
                            p.dma(at[:, :], accd[t])
                        for g2 in range(2):
                            p.mm(ps_o[:, :], attnTh[:, t * 128:(t + 1) * 128], wo[:, h, g2 * 512:(g2 + 1) * 512], start=True, stop=True)
                            p.stt(at[:, g2 * 512:(g2 + 1) * 512], ps_o[:, :], rcp[:, t:t + 1], at[:, g2 * 512:(g2 + 1) * 512], "mult", "add")
                        p.dma(accd[t], at[:, :])
                for t in range(NT):
                    p.dma(at[:, :], accd[t])
                    p.dma(c.t0[:, :], xin[bi, t * 128:(t + 1) * 128, :])
                    p.tt(at[:, :], at[:, :], bo32[:, :], "add")
                    p.tt(at[:, :], at[:, :], c.t0[:, :], "add")
                    p.dma(xout[bi, t * 128:(t + 1) * 128, :], at[:, :])


def emit_l1attna(p, c, xin, xout, Wd, tag):
    B = c.B
    aTd = p.dram(tag + "aTd", (8, 128, SEQ), BF16)
    with p.scope():
        nw = p.sb("nw", (128, 8), F32)
        p.dma(nw[:, :], Wd["a1_nw_pg"][:, :])
        wq = p.sb("a1wq", (128, 8, 1024), BF16)
        wk = p.sb("a1wk", (128, 8, 512), BF16)
        wv = p.sb("a1wv", (128, 8, 512), BF16)
        wo = loadw_bf(p, c, "a1wo", Wd["a1_Wo"], 8, 1024)
        wf = p.sb("a1wf", (128, 1024), F32)
        for k in range(8):
            p.dma(wf[:, :], Wd["a1_Wq"][k * 128:(k + 1) * 128, :])
            p.ts(wf[:, :], wf[:, :], nw[:, k:k + 1], "mult")
            p.copy(wq[:, k, :], wf[:, :], q="vector")
            p.dma(wf[:, 0:512], Wd["a1_Wk"][k * 128:(k + 1) * 128, :])
            p.ts(wf[:, 0:512], wf[:, 0:512], nw[:, k:k + 1], "mult")
            p.copy(wk[:, k, :], wf[:, 0:512], q="vector")
            p.dma(wf[:, 0:512], Wd["a1_Wv"][k * 128:(k + 1) * 128, :])
            p.ts(wf[:, 0:512], wf[:, 0:512], nw[:, k:k + 1], "mult")
            p.copy(wv[:, k, :], wf[:, 0:512], q="vector")
        tmask = p.sb("tmask", (128, 128), F32)
        p.dma(tmask[:, :], Wd["c_tmask"][:, :])
        scol = scalar_col(p, c, "tempcol", Wd["a1_ltemp"], fn=lambda s: p.act(s[:, :], s[:, :], AF.Exp), slot=0)
        gcol = scalar_col(p, c, "gatecol", Wd["a1_gate"], slot=1)

        qT = p.sb("a1qT", (128, 8, SEQ), BF16)
        kTn = p.sb("a1kT", (128, 4, SEQ), BF16)
        vtm = p.sb("a1v", (128, NT, 512), BF16)
        qtm = p.sb("a1q", (128, 1024), F32)
        lb = p.sb("a1lb", (128, 1024), BF16)
        sc = p.sb("a1sc", (128, SEQ), F32)
        scb = p.sb("a1scb", (128, SEQ), BF16)
        PT = p.sb("a1PT", (128, NT, 128), BF16)
        ps1 = p.ps("a1ps1", (128, 512), F32)
        ps2 = p.ps("a1ps2", (128, 512), F32)

        for bi in range(B):
            with p.scope():
                hT = p.sb("a1hT", (128, 8, SEQ), BF16)
                for t in range(NT):
                    p.dma(c.t0[:, :], xin[bi, t * 128:(t + 1) * 128, :])
                    ln_tile(p, c, c.t0[:, :], c.t0[:, :])
                    p.act(c.t1[:, :], c.t0[:, :], AF.Square)
                    p.red(c.c0[:, :], c.t1[:, :], "add")
                    p.ts(c.c0[:, :], c.c0[:, :], 1.0 / D, "mult")
                    p.ts(c.c0[:, :], c.c0[:, :], 1e-6, "add")
                    p.act(c.c0[:, :], c.c0[:, :], AF.Ln)
                    p.ts(c.c0[:, :], c.c0[:, :], -0.5, "mult")
                    p.act(c.c0[:, :], c.c0[:, :], AF.Exp)
                    p.ts(c.t0[:, :], c.t0[:, :], c.c0[:, :], "mult")
                    p.copy(lb[:, :], c.t0[:, :], q="vector")
                    for g in range(8):
                        tr128(p, c, hT[:, g, t * 128:(t + 1) * 128], lb[:, g * 128:(g + 1) * 128], 128, 128)
                for t in range(NT):
                    for g2 in range(2):
                        for k in range(8):
                            p.mm(ps1[:, :], hT[:, k, t * 128:(t + 1) * 128], wq[:, k, g2 * 512:(g2 + 1) * 512], start=(k == 0), stop=(k == 7))
                        p.copy(qtm[:, g2 * 512:(g2 + 1) * 512], ps1[:, :])
                    p.act(c.t1[:, :], qtm[:, :], AF.Square)
                    for hh in range(4):
                        p.red(c.c0[:, :], c.t1[:, hh * 256:(hh + 1) * 256], "add")
                        p.act(c.c0[:, :], c.c0[:, :], AF.Sqrt)
                        p.ts(c.c0[:, :], c.c0[:, :], 1e-12, "max")
                        p.op("vector", lambda e: e.reciprocal(c.c0[:, :], c.c0[:, :]))
                        p.ts(qtm[:, hh * 256:(hh + 1) * 256], qtm[:, hh * 256:(hh + 1) * 256], c.c0[:, :], "mult")
                    p.copy(lb[:, :], qtm[:, :], q="vector")
                    for g in range(8):
                        tr128(p, c, qT[:, g, t * 128:(t + 1) * 128], lb[:, g * 128:(g + 1) * 128], 128, 128)
                for t in range(NT):
                    for k in range(8):
                        p.mm(ps1[:, :], hT[:, k, t * 128:(t + 1) * 128], wk[:, k, :], start=(k == 0), stop=(k == 7))
                    p.copy(qtm[:, 0:512], ps1[:, :])
                    p.act(c.t1[:, 0:512], qtm[:, 0:512], AF.Square)
                    for hh in range(2):
                        p.red(c.c0[:, :], c.t1[:, hh * 256:(hh + 1) * 256], "add")
                        p.act(c.c0[:, :], c.c0[:, :], AF.Sqrt)
                        p.ts(c.c0[:, :], c.c0[:, :], 1e-12, "max")
                        p.op("vector", lambda e: e.reciprocal(c.c0[:, :], c.c0[:, :]))
                        p.ts(qtm[:, hh * 256:(hh + 1) * 256], qtm[:, hh * 256:(hh + 1) * 256], c.c0[:, :], "mult")
                    p.copy(lb[:, 0:512], qtm[:, 0:512], q="vector")
                    for g in range(4):
                        tr128(p, c, kTn[:, g, t * 128:(t + 1) * 128], lb[:, g * 128:(g + 1) * 128], 128, 128)
                    for k in range(8):
                        p.mm(ps1[:, :], hT[:, k, t * 128:(t + 1) * 128], wv[:, k, :], start=(k == 0), stop=(k == 7))
                    p.copy(vtm[:, t, :], ps1[:, :], q="vector")
            for h in range(4):
                kv = h // 2
                for qt in range(NT):
                    kl = (qt + 1) * 128
                    for g2 in range((kl + 511) // 512):
                        f0 = g2 * 512
                        fl = min(512, kl - f0)
                        for j in range(2):
                            p.mm(ps1[:, 0:fl], qT[:, 2 * h + j, qt * 128:(qt + 1) * 128], kTn[:, 2 * kv + j, f0:f0 + fl], start=(j == 0), stop=(j == 1))
                        p.copy(sc[:, f0:f0 + fl], ps1[:, 0:fl])
                    p.ts(sc[:, 0:kl], sc[:, 0:kl], scol[:, :], "mult")
                    p.tt(sc[:, qt * 128:kl], sc[:, qt * 128:kl], tmask[:, :], "add")
                    p.red(c.c0[:, :], sc[:, 0:kl], "max", negate=True)
                    p.act(sc[:, 0:kl], sc[:, 0:kl], AF.Exp, bias=c.c0[:, :])
                    p.red(c.c1[:, :], sc[:, 0:kl], "add")
                    p.op("vector", lambda e: e.reciprocal(c.c1[:, :], c.c1[:, :]))
                    p.ts(sc[:, 0:kl], sc[:, 0:kl], c.c1[:, :], "mult")
                    p.copy(scb[:, 0:kl], sc[:, 0:kl], q="vector")
                    for kt in range(qt + 1):
                        tr128(p, c, PT[:, kt, :], scb[:, kt * 128:(kt + 1) * 128], 128, 128)
                    for j in range(2):
                        for kt in range(qt + 1):
                            p.mm(ps2[:, 0:128], vtm[:, kt, kv * 256 + j * 128: kv * 256 + (j + 1) * 128], PT[:, kt, :], start=(kt == 0), stop=(kt == qt))
                        p.copy(ev2 := sc[:, 0:128], ps2[:, 0:128])
                        p.copy(lb[:, 0:128], ev2, q="vector")
                        p.dma(aTd[2 * h + j, :, qt * 128:(qt + 1) * 128], lb[:, 0:128])
            with p.scope():
                aT = p.sb("a1aT", (128, 8, SEQ), BF16)
                lnt = p.sb("a1lnt", (128, 1024), F32)
                for g in range(8):
                    p.dma(aT[:, g, :], aTd[g])
                for t in range(NT):
                    p.dma(c.t0[:, :], xin[bi, t * 128:(t + 1) * 128, :])
                    ln_tile(p, c, lnt[:, :], c.t0[:, :])
                    p.tt(c.t0[:, :], c.t0[:, :], lnt[:, :], "add")
                    for g2 in range(2):
                        for k in range(8):
                            p.mm(ps1[:, :], aT[:, k, t * 128:(t + 1) * 128], wo[:, k, g2 * 512:(g2 + 1) * 512], start=(k == 0), stop=(k == 7))
                        p.stt(c.t0[:, g2 * 512:(g2 + 1) * 512], ps1[:, :], gcol[:, :], c.t0[:, g2 * 512:(g2 + 1) * 512], "mult", "add")
                    p.dma(xout[bi, t * 128:(t + 1) * 128, :], c.t0[:, :])

def emit_l0ffn(p, c, xin, xout, Wd, tag):
    B = c.B
    with p.scope():
        w1aT = loadw_bf(p, c, "w1aT", Wd["f0_w1aT"], 8, 512)
        w2T = loadw_bf(p, c, "w2T", Wd["f0_w2T"], 4, 64)
        wvT = loadw_bf(p, c, "wvT", Wd["f0_wvT"], 8, 1024)
        woT = loadw_bf(p, c, "woT", Wd["f0_woT"], 8, 1024)
        b2b, _ = loadrow_bf(p, "b2r", Wd["f0_b2"], 64)
        bvb, _ = loadrow_bf(p, "bvr0", Wd["f0_bv"], 1024)
        bob, _ = loadrow_bf(p, "bor0", Wd["f0_bo"], 1024)
        def taufn(s):
            p.act(s[:, :], s[:, :], AF.Abs)
            p.ts(s[:, :], s[:, :], 0.1, "add")
            p.op("vector", lambda e: e.reciprocal(s[:, :], s[:, :]))
        taucol = scalar_col(p, c, "taucol", Wd["f0_tau"], fn=taufn, slot=2)
        S = p.sb("slotS", (64, 1024), F32)
        scl = p.sb("sscale", (64, 1), F32)
        p.dma(S[:, :], Wd["f0_sinit"][:, :])
        p.dma(scl[:, :], Wd["f0_sscale"][:, :])
        p.ts(S[:, :], S[:, :], scl[:, :], "mult")
        Sb = p.sb("slotSb", (64, 1024), BF16)
        p.copy(Sb[:, :], S[:, :], q="vector")
        gh = p.sb("gh", (64, 3072), F32)
        with p.scope():
            whhT = loadw_bf(p, c, "whhT", Wd["f0_whhT"], 8, 3072)
            bhhb, _ = loadrow_bf(p, "bhhr", Wd["f0_bhh"], 3072)
            hsT = p.sb("hsT", (128, 8, 64), BF16)
            for k in range(8):
                tr128(p, c, hsT[:, k, :], Sb[:, k * 128:(k + 1) * 128], 64, 128)
            psg = p.ps("psg", (64, 512), F32)
            for grp in range(6):
                for k in range(8):
                    p.mm(psg[:, :], hsT[:, k, :], whhT[:, k, grp * 512:(grp + 1) * 512], start=(k == 0), stop=False)
                p.mm(psg[:, :], c.ones1b[:, 0:64], bhhb[:, grp * 512:(grp + 1) * 512], start=False, stop=True)
                p.copy(gh[:, grp * 512:(grp + 1) * 512], psg[:, :])
        biasTc = p.sb("biasTc", (128, 4), F32)
        with p.scope():
            w1bT = loadw_bf(p, c, "w1bT", Wd["f0_w1bT"], 8, 512)
            psm = p.ps("psm", (64, 512), F32)
            sm = p.sb("sm", (1, 1024), F32)
            for grp in range(2):
                p.mm(psm[0:1, :], c.onescolb[0:64, :], Sb[:, grp * 512:(grp + 1) * 512], start=True, stop=True)
                p.act(sm[:, grp * 512:(grp + 1) * 512], psm[0:1, :], AF.Identity, scale=1.0 / 64.0)
            smT = p.sb("smT", (128, 8, 1), F32)
            smTb = p.sb("smTb", (128, 8, 1), BF16)
            for k in range(8):
                tr128(p, c, smT[:, k, :], sm[0:1, k * 128:(k + 1) * 128], 1, 128, fp32=True)
            p.copy(smTb[:, :, :], smT[:, :, :], q="vector")
            br = p.sb("brow", (1, 512), F32)
            b1f = p.sb("b1f", (1, 512), F32)
            p.dma(b1f[:, :], Wd["f0_b1"][:, :])
            for k in range(8):
                p.mm(psm[0:1, :], smTb[:, k, :], w1bT[:, k, :], start=(k == 0), stop=(k == 7))
            p.copy(br[:, :], psm[0:1, :])
            p.tt(br[:, :], br[:, :], b1f[:, :], "add")
            for j in range(4):
                tr128(p, c, biasTc[:, j:j + 1], br[0:1, j * 128:(j + 1) * 128], 1, 128, fp32=True)

        alphaT = p.sb("alphaT", (64, NT, 128), BF16)
        gi = p.sb("gi", (64, 1024), F32)
        svb = p.sb("svb", (64, 1024), BF16)
        lg = p.sb("lg0", (128, 64), F32)
        wkk = p.sb("wk0", (128, 64), F32)
        kp = p.sb("kp0", (128, 64), F32)
        mx = p.sb("mx0", (128, 8), F32)
        alb = p.sb("alb", (128, 64), BF16)
        lnb = p.sb("lnb0", (128, 1024), BF16)
        xTt = p.sb("xTt0", (128, 8, 128), BF16)
        hTb = p.sb("hTb0", (128, 4, 128), BF16)

        for bi in range(B):
            with p.scope():
                si0 = p.ps("si0", (64, 512), F32)
                si1 = p.ps("si1", (64, 512), F32)
                ws = p.ps("ws0", (64, 1), F32)
                ps_h = p.ps("ps_h0", (128, 128), F32)
                ps_lg = p.ps("ps_lg", (128, 64), F32)
                for t in range(NT):
                    p.dma(c.t0[:, :], xin[bi, t * 128:(t + 1) * 128, :])
                    ln_tile(p, c, c.t0[:, :], c.t0[:, :])
                    p.copy(lnb[:, :], c.t0[:, :], q="vector")
                    for g in range(8):
                        tr128(p, c, xTt[:, g, :], lnb[:, g * 128:(g + 1) * 128], 128, 128)
                    for ht in range(4):
                        for k in range(8):
                            p.mm(ps_h[:, :], w1aT[:, k, ht * 128:(ht + 1) * 128], xTt[:, k, :], start=(k == 0), stop=(k == 7))
                        p.act(hTb[:, ht, :], ps_h[:, :], AF.Gelu, bias=biasTc[:, ht:ht + 1])
                    for k in range(4):
                        p.mm(ps_lg[:, :], hTb[:, k, :], w2T[:, k, :], start=(k == 0), stop=False)
                    p.mm(ps_lg[:, :], c.ones1b[:, :], b2b[:, :], start=False, stop=True)
                    p.copy(lg[:, :], ps_lg[:, :])
                    p.ts(lg[:, :], lg[:, :], taucol[:, :], "mult")
                    p.copy(wkk[:, :], lg[:, :], q="vector")
                    for i in range(8):
                        p.red(mx[:, i:i + 1], wkk[:, :], "max")
                        if i < 7:
                            p.ts(kp[:, :], wkk[:, :], mx[:, i:i + 1], "is_ge")
                            p.stt(wkk[:, :], kp[:, :], -1e30, wkk[:, :], "mult", "add")
                    p.ts(c.c0[:, :], mx[:, 0:1], -1.0, "mult")
                    p.ts(kp[:, :], lg[:, :], mx[:, 7:8], "is_ge")
                    p.act(lg[:, :], lg[:, :], AF.Exp, bias=c.c0[:, :])
                    p.tt(lg[:, :], lg[:, :], kp[:, :], "mult")
                    p.red(c.c1[:, :], lg[:, :], "add")
                    p.op("vector", lambda e: e.reciprocal(c.c1[:, :], c.c1[:, :]))
                    p.ts(lg[:, :], lg[:, :], c.c1[:, :], "mult")
                    p.copy(alb[:, :], lg[:, :], q="vector")
                    tr128(p, c, alphaT[:, t, :], alb[:, :], 128, 64)
                    p.mm(si0[:, :], alb[:, :], lnb[:, 0:512], start=(t == 0), stop=(t == NT - 1))
                    p.mm(si1[:, :], alb[:, :], lnb[:, 512:1024], start=(t == 0), stop=(t == NT - 1))
                    p.mm(ws[:, :], alb[:, :], c.onescolb[:, :], start=(t == 0), stop=(t == NT - 1))
                p.copy(gi[:, 0:512], si0[:, :])
                p.copy(gi[:, 512:1024], si1[:, :])
                wse = p.sb("wse", (64, 1), F32)
                p.copy(wse[:, :], ws[:, :])
                p.ts(wse[:, :], wse[:, :], 1e-8, "add")
                p.op("vector", lambda e: e.reciprocal(wse[:, :], wse[:, :]))
                p.ts(gi[:, :], gi[:, :], wse[:, :], "mult")
            with p.scope():
                gib = p.sb("gib", (64, 1024), BF16)
                p.copy(gib[:, :], gi[:, :], q="vector")
                giT = p.sb("giT", (128, 8, 64), BF16)
                for k in range(8):
                    tr128(p, c, giT[:, k, :], gib[:, k * 128:(k + 1) * 128], 64, 128)
                wihT = loadw_bf(p, c, "wihT", Wd["f0_wihT"], 8, 3072)
                bihb, _ = loadrow_bf(p, "bihr", Wd["f0_bih"], 3072)
                gx = p.sb("gx", (64, 3072), F32)
                psg2 = p.ps("psg2", (64, 512), F32)
                for grp in range(6):
                    for k in range(8):
                        p.mm(psg2[:, :], giT[:, k, :], wihT[:, k, grp * 512:(grp + 1) * 512], start=(k == 0), stop=False)
                    p.mm(psg2[:, :], c.ones1b[:, 0:64], bihb[:, grp * 512:(grp + 1) * 512], start=False, stop=True)
                    p.copy(gx[:, grp * 512:(grp + 1) * 512], psg2[:, :])
                rr = p.sb("rr", (64, 1024), F32)
                zz = p.sb("zz", (64, 1024), F32)
                nn = p.sb("nn", (64, 1024), F32)
                tmp = p.sb("tmpg", (64, 1024), F32)
                p.tt(rr[:, :], gx[:, 0:1024], gh[:, 0:1024], "add")
                p.act(rr[:, :], rr[:, :], AF.Sigmoid)
                p.tt(zz[:, :], gx[:, 1024:2048], gh[:, 1024:2048], "add")
                p.act(zz[:, :], zz[:, :], AF.Sigmoid)
                p.tt(tmp[:, :], rr[:, :], gh[:, 2048:3072], "mult")
                p.tt(nn[:, :], gx[:, 2048:3072], tmp[:, :], "add")
                p.act(nn[:, :], nn[:, :], AF.Tanh)
                p.tt(tmp[:, :], S[:, :], nn[:, :], "subtract")
                p.tt(tmp[:, :], zz[:, :], tmp[:, :], "mult")
                p.tt(nn[:, :], nn[:, :], tmp[:, :], "add")
                hnb = p.sb("hnb", (64, 1024), BF16)
                p.copy(hnb[:, :], nn[:, :], q="vector")
                hnT = p.sb("hnT", (128, 8, 64), BF16)
                for k in range(8):
                    tr128(p, c, hnT[:, k, :], hnb[:, k * 128:(k + 1) * 128], 64, 128)
                for grp in range(2):
                    for k in range(8):
                        p.mm(psg2[:, :], hnT[:, k, :], wvT[:, k, grp * 512:(grp + 1) * 512], start=(k == 0), stop=False)
                    p.mm(psg2[:, :], c.ones1b[:, 0:64], bvb[:, grp * 512:(grp + 1) * 512], start=False, stop=True)
                    p.copy(svb[:, grp * 512:(grp + 1) * 512], psg2[:, :], q="vector")
            with p.scope():
                ps_o = p.ps("ps_o0", (128, 512), F32)
                ps_w = p.ps("ps_w0", (128, 512), F32)
                obf = p.sb("obf", (128, 1024), BF16)
                oTt = p.sb("oTt", (128, 8, 128), BF16)
                for t in range(NT):
                    for g2 in range(2):
                        p.mm(ps_o[:, :], alphaT[:, t, :], svb[:, g2 * 512:(g2 + 1) * 512], start=True, stop=True)
                        p.copy(obf[:, g2 * 512:(g2 + 1) * 512], ps_o[:, :], q="vector")
                    for g in range(8):
                        tr128(p, c, oTt[:, g, :], obf[:, g * 128:(g + 1) * 128], 128, 128)
                    p.dma(c.t0[:, :], xin[bi, t * 128:(t + 1) * 128, :])
                    for g2 in range(2):
                        for k in range(8):
                            p.mm(ps_w[:, :], oTt[:, k, :], woT[:, k, g2 * 512:(g2 + 1) * 512], start=(k == 0), stop=False)
                        p.mm(ps_w[:, :], c.ones1b[:, :], bob[:, g2 * 512:(g2 + 1) * 512], start=False, stop=True)
                        p.tt(c.t0[:, g2 * 512:(g2 + 1) * 512], ps_w[:, :], c.t0[:, g2 * 512:(g2 + 1) * 512], "add")
                    p.dma(xout[bi, t * 128:(t + 1) * 128, :], c.t0[:, :])


MAGIC = 12582912.0
INV2PI = 1.0 / (2.0 * np.pi)
TWOPI = 2.0 * np.pi


def emit_l0attnb(p, c, xin, xout, Wd, tag):
    B = c.B
    Q = 64
    NCH = SEQ // Q
    ufm = p.dram(tag + "ufm", (B, 8, 128, SEQ), F32)
    ygel = p.dram(tag + "ygel", (B, 8, 128, SEQ), BF16)
    with p.scope():
        ev = p.sb("ev0", (128, 128), F32)
        for bi in range(B):
            for t in range(NT):
                p.dma(c.t0[:, :], xin[bi, t * 128:(t + 1) * 128, :])
                ln_tile(p, c, c.t0[:, :], c.t0[:, :])
                for g in range(8):
                    tr128(p, c, ev[:, :], c.t0[:, g * 128:(g + 1) * 128], 128, 128, fp32=True)
                    p.dma(ufm[bi, g, :, t * 128:(t + 1) * 128], ev[:, :])
    with p.scope():
        G = lambda name: p.sb(name, (128, 8, 32), F32)
        dt = p.sb("dt", (128, 8), F32)
        p.dma(dt[:, :], Wd["b0_logdt_pg"][:, :])
        p.act(dt[:, :], dt[:, :], AF.Exp)
        we = p.sb("we", (128, 32), F32)
        wim = p.sb("wim", (128, 32), F32)
        p.dma(we[:, :], Wd["b0_lwr"].broadcast_to([128, 32]))
        p.act(we[:, :], we[:, :], AF.Exp)
        p.dma(wim[:, :], Wd["b0_wim"].broadcast_to([128, 32]))
        dtv = dt.rearrange("pp (g o) -> pp g o", o=1).broadcast_to([128, 8, 32])
        wev = we.rearrange("pp (o n) -> pp o n", o=1).broadcast_to([128, 8, 32])
        wimv = wim.rearrange("pp (o n) -> pp o n", o=1).broadcast_to([128, 8, 32])
        dAr, dAi, E, sinb, cosb = G("dAr"), G("dAi"), G("E"), G("sinb"), G("cosb")
        t1g, t2g = G("t1g"), G("t2g")
        p.tt(dAr[:, :, :], dtv, wev, "mult")
        p.ts(dAr[:, :, :], dAr[:, :, :], -1.0, "mult")
        p.tt(dAi[:, :, :], dtv, wimv, "mult")
        p.act(E[:, :, :], dAr[:, :, :], AF.Exp)

        def redsin(dst, src_done_in_t1g):
            p.ts(t2g[:, :, :], t1g[:, :, :], INV2PI, "mult")
            p.ts(t2g[:, :, :], t2g[:, :, :], MAGIC, "add")
            p.ts(t2g[:, :, :], t2g[:, :, :], -MAGIC, "add")
            p.ts(t2g[:, :, :], t2g[:, :, :], -TWOPI, "mult")
            p.stt(t2g[:, :, :], t1g[:, :, :], 1.0, t2g[:, :, :], "mult", "add")
            p.act(dst, t2g[:, :, :], AF.Sin)

        p.copy(t1g[:, :, :], dAi[:, :, :], q="vector")
        redsin(sinb[:, :, :], None)
        p.ts(t1g[:, :, :], dAi[:, :, :], np.pi / 2.0, "add")
        redsin(cosb[:, :, :], None)
        zr, zi = G("zr"), G("zi")
        p.tt(zr[:, :, :], E[:, :, :], cosb[:, :, :], "mult")
        p.tt(zi[:, :, :], E[:, :, :], sinb[:, :, :], "mult")
        Cr, Ci = G("Cr"), G("Ci")
        p.dma(Cr[:, :, :], Wd["b0_Cr"].rearrange("(g pp) n -> pp g n", pp=128))
        p.dma(Ci[:, :, :], Wd["b0_Ci"].rearrange("(g pp) n -> pp g n", pp=128))
        den = p.sb("den", (128, 32), F32)
        p.act(den[:, :], we[:, :], AF.Square)
        p.act(t2g[:, 0, :], wim[:, :], AF.Square)
        p.tt(den[:, :], den[:, :], t2g[:, 0, :], "add")
        p.op("vector", lambda e: e.reciprocal(den[:, :], den[:, :]))
        denv = den.rearrange("pp (o n) -> pp o n", o=1).broadcast_to([128, 8, 32])
        em1r = t1g
        p.ts(em1r[:, :, :], zr[:, :, :], 1.0, "subtract")
        qr, qi = G("qr"), G("qi")
        p.tt(qr[:, :, :], em1r[:, :, :], wev, "mult")
        p.tt(t2g[:, :, :], zi[:, :, :], wimv, "mult")
        p.tt(qr[:, :, :], t2g[:, :, :], qr[:, :, :], "subtract")
        p.tt(qr[:, :, :], qr[:, :, :], denv, "mult")
        p.tt(qi[:, :, :], em1r[:, :, :], wimv, "mult")
        p.tt(t2g[:, :, :], zi[:, :, :], wev, "mult")
        p.tt(qi[:, :, :], qi[:, :, :], t2g[:, :, :], "add")
        p.tt(qi[:, :, :], qi[:, :, :], denv, "mult")
        p.ts(qi[:, :, :], qi[:, :, :], -1.0, "mult")
        Cwr, Cwi, nCwi = G("Cwr"), G("Cwi"), G("nCwi")
        p.tt(Cwr[:, :, :], Cr[:, :, :], qr[:, :, :], "mult")
        p.tt(t2g[:, :, :], Ci[:, :, :], qi[:, :, :], "mult")
        p.tt(Cwr[:, :, :], Cwr[:, :, :], t2g[:, :, :], "subtract")
        p.tt(Cwi[:, :, :], Cr[:, :, :], qi[:, :, :], "mult")
        p.tt(t2g[:, :, :], Ci[:, :, :], qr[:, :, :], "mult")
        p.tt(Cwi[:, :, :], Cwi[:, :, :], t2g[:, :, :], "add")
        p.ts(Cwr[:, :, :], Cwr[:, :, :], 2.0, "mult")
        p.ts(Cwi[:, :, :], Cwi[:, :, :], 2.0, "mult")
        p.ts(nCwi[:, :, :], Cwi[:, :, :], -1.0, "mult")
        Dcol = p.sb("Dcol", (128, 8), F32)
        p.dma(Dcol[:, :], Wd["b0_D_pg"][:, :])

        Zpr = p.sb("Zpr", (128, 32, 65), F32)
        Zpi = p.sb("Zpi", (128, 32, 65), F32)
        Zrr = p.sb("Zrr", (128, 32, 64), F32)
        Zri = p.sb("Zri", (128, 32, 64), F32)
        ztmp = p.sb("ztmp", (128, 32, 32), F32)
        ztmp2 = p.sb("ztmp2", (128, 32, 32), F32)
        kg = p.sb("kg", (128, 64), F32)
        NB = min(2, B)
        u_all = p.sb("u_all", (128, NB, SEQ), F32)
        y_all = p.sb("y_all", (128, NB, SEQ), F32)
        big = p.sb("big", (128, NB, SEQ), F32)
        Pr = p.sb("Pr", (128, 32, NB, NCH), F32)
        Pi = p.sb("Pi", (128, 32, NB, NCH), F32)
        Sar = p.sb("Sar", (128, 32, NB, NCH), F32)
        Sai = p.sb("Sai", (128, 32, NB, NCH), F32)
        Sr = p.sb("Sr", (128, 32, NB), F32)
        Si = p.sb("Si", (128, 32, NB), F32)
        sc1 = p.sb("sc1", (128, 32, NB), F32)
        sc2 = p.sb("sc2", (128, 32, NB), F32)
        ygb = p.sb("ygb", (128, SEQ), BF16)
        for bh, g in [(bh, g) for bh in range(0, B, NB) for g in range(8)]:
                p.memset(Zpr[:, :, 0:1], 1.0)
                p.memset(Zpi[:, :, 0:1], 0.0)
                p.copy(Zpr[:, :, 1:2], zr[:, g, :].rearrange("pp (n o) -> pp n o", o=1), q="vector")
                p.copy(Zpi[:, :, 1:2], zi[:, g, :].rearrange("pp (n o) -> pp n o", o=1), q="vector")
                for m in [1, 2, 4, 8, 16, 32]:
                    br_ = Zpr[:, :, m:m + 1].broadcast_to([128, 32, m])
                    bi_ = Zpi[:, :, m:m + 1].broadcast_to([128, 32, m])
                    p.tt(ztmp[:, :, 0:m], Zpr[:, :, 1:m + 1], br_, "mult")
                    p.tt(ztmp2[:, :, 0:m], Zpi[:, :, 1:m + 1], bi_, "mult")
                    p.tt(Zpr[:, :, m + 1:2 * m + 1], ztmp[:, :, 0:m], ztmp2[:, :, 0:m], "subtract")
                    p.tt(ztmp[:, :, 0:m], Zpr[:, :, 1:m + 1], bi_, "mult")
                    p.tt(ztmp2[:, :, 0:m], Zpi[:, :, 1:m + 1], br_, "mult")
                    p.tt(Zpi[:, :, m + 1:2 * m + 1], ztmp[:, :, 0:m], ztmp2[:, :, 0:m], "add")
                for i in range(64):
                    p.copy(Zrr[:, :, i:i + 1], Zpr[:, :, 63 - i:64 - i], q="vector")
                    p.copy(Zri[:, :, i:i + 1], Zpi[:, :, 63 - i:64 - i], q="vector")
                p.memset(kg[:, :], 0.0)
                for n in range(32):
                    p.stt(kg[:, :], Zpr[:, n, 0:64], Cwr[:, g, n:n + 1], kg[:, :], "mult", "add")
                    p.stt(kg[:, :], Zpi[:, n, 0:64], nCwi[:, g, n:n + 1], kg[:, :], "mult", "add")
                if getattr(p, "dbg_b0", False) and bh == 0:
                    ko_ = p.out(tag + "_dbg_kg%d" % g, (128, 64), F32)
                    p.dma(ko_, kg[:, :])
                for bi in range(NB):
                    p.dma(u_all[:, bi, :], ufm[bh + bi, g])
                uv = u_all.rearrange("pp b (cc i) -> pp b cc i", i=Q)
                yv = y_all.rearrange("pp b (cc i) -> pp b cc i", i=Q)
                p.ts(yv[:, :, :, :], uv[:, :, :, :], kg[:, 0:1], "mult")
                for d in range(1, 64):
                    p.stt(yv[:, :, :, d:], uv[:, :, :, 0:Q - d], kg[:, d:d + 1], yv[:, :, :, d:], "mult", "add")
                bigv = big.rearrange("pp b (cc i) -> pp b cc i", i=Q)
                for n in range(32):
                    zv = Zrr[:, n, :].rearrange("pp (o i) -> pp o i", o=1).rearrange("pp o (oo i) -> pp o oo i", oo=1).broadcast_to([128, NB, NCH, Q])
                    p.tt(bigv[:, :, :, :], uv[:, :, :, :], zv, "mult")
                    p.red(Pr[:, n, :, :], bigv[:, :, :, :], "add")
                    zv2 = Zri[:, n, :].rearrange("pp (o i) -> pp o i", o=1).rearrange("pp o (oo i) -> pp o oo i", oo=1).broadcast_to([128, NB, NCH, Q])
                    p.tt(bigv[:, :, :, :], uv[:, :, :, :], zv2, "mult")
                    p.red(Pi[:, n, :, :], bigv[:, :, :, :], "add")
                p.memset(Sr[:, :, :], 0.0)
                p.memset(Si[:, :, :], 0.0)
                zQr = Zpr[:, :, 64:65].broadcast_to([128, 32, NB])
                zQi = Zpi[:, :, 64:65].broadcast_to([128, 32, NB])
                for ch in range(NCH):
                    p.copy(Sar[:, :, :, ch], Sr[:, :, :], q="vector")
                    p.copy(Sai[:, :, :, ch], Si[:, :, :], q="vector")
                    p.tt(sc1[:, :, :], Sr[:, :, :], zQr, "mult")
                    p.tt(sc2[:, :, :], Si[:, :, :], zQi, "mult")
                    p.tt(sc1[:, :, :], sc1[:, :, :], sc2[:, :, :], "subtract")
                    p.tt(sc2[:, :, :], Si[:, :, :], zQr, "mult")
                    p.tt(Si[:, :, :], Sr[:, :, :], zQi, "mult")
                    p.tt(Si[:, :, :], Si[:, :, :], sc2[:, :, :], "add")
                    p.tt(Si[:, :, :], Si[:, :, :], Pi[:, :, :, ch], "add")
                    p.tt(Sr[:, :, :], sc1[:, :, :], Pr[:, :, :, ch], "add")
                Cwrv = Cwr[:, g, :].rearrange("pp (n o) -> pp n o", o=1).rearrange("pp n (o oo) -> pp n o oo", oo=1).broadcast_to([128, 32, NB, NCH])
                Cwiv = Cwi[:, g, :].rearrange("pp (n o) -> pp n o", o=1).rearrange("pp n (o oo) -> pp n o oo", oo=1).broadcast_to([128, 32, NB, NCH])
                p.tt(Pr[:, :, :, :], Sar[:, :, :, :], Cwrv, "mult")
                p.tt(Pi[:, :, :, :], Sai[:, :, :, :], Cwiv, "mult")
                p.tt(Pr[:, :, :, :], Pr[:, :, :, :], Pi[:, :, :, :], "subtract")
                p.tt(Pi[:, :, :, :], Sar[:, :, :, :], Cwiv, "mult")
                p.tt(Sar[:, :, :, :], Sai[:, :, :, :], Cwrv, "mult")
                p.tt(Pi[:, :, :, :], Pi[:, :, :, :], Sar[:, :, :, :], "add")
                for n in range(32):
                    Arv = Pr[:, n, :, :].rearrange("pp b (cc o) -> pp b cc o", o=1).broadcast_to([128, NB, NCH, Q])
                    Aiv = Pi[:, n, :, :].rearrange("pp b (cc o) -> pp b cc o", o=1).broadcast_to([128, NB, NCH, Q])
                    z1r = Zpr[:, n, 1:65].rearrange("pp (o i) -> pp o i", o=1).rearrange("pp o (oo i) -> pp o oo i", oo=1).broadcast_to([128, NB, NCH, Q])
                    z1i = Zpi[:, n, 1:65].rearrange("pp (o i) -> pp o i", o=1).rearrange("pp o (oo i) -> pp o oo i", oo=1).broadcast_to([128, NB, NCH, Q])
                    p.tt(bigv[:, :, :, :], Arv, z1r, "mult")
                    p.tt(yv[:, :, :, :], yv[:, :, :, :], bigv[:, :, :, :], "add")
                    p.tt(bigv[:, :, :, :], Aiv, z1i, "mult")
                    p.tt(yv[:, :, :, :], yv[:, :, :, :], bigv[:, :, :, :], "subtract")
                for bi in range(NB):
                    p.stt(y_all[:, bi, :], u_all[:, bi, :], Dcol[:, g:g + 1], y_all[:, bi, :], "mult", "add")
                    p.act(y_all[:, bi, :], y_all[:, bi, :], AF.Gelu)
                    p.copy(ygb[:, :], y_all[:, bi, :], q="vector")
                    p.dma(ygel[bh + bi, g], ygb[:, :])
    if getattr(p, "dbg_b0", False):
        yo_ = p.out(tag + "_dbg_ygel", (B, 8, 128, SEQ), BF16)
        with p.scope():
            tmpy = p.sb("tmpy", (128, SEQ), BF16)
            for bi in range(B):
                for g in range(8):
                    p.dma(tmpy[:, :], ygel[bi, g])
                    p.dma(yo_[bi, g], tmpy[:, :])
    with p.scope():
        woT = loadw_bf(p, c, "b0woT", Wd["b0_WoT"], 8, 1024)
        bob, _ = loadrow_bf(p, "b0bor", Wd["b0_bo"], 1024)
        yg = p.sb("yg", (128, 8, SEQ), BF16)
        ps_w = p.ps("ps_wb0", (128, 512), F32)
        for bi in range(B):
            for g in range(8):
                p.dma(yg[:, g, :], ygel[bi, g])
            for t in range(NT):
                p.dma(c.t0[:, :], xin[bi, t * 128:(t + 1) * 128, :])
                for g2 in range(2):
                    for k in range(8):
                        p.mm(ps_w[:, :], yg[:, k, t * 128:(t + 1) * 128], woT[:, k, g2 * 512:(g2 + 1) * 512], start=(k == 0), stop=False)
                    p.mm(ps_w[:, :], c.ones1b[:, :], bob[:, g2 * 512:(g2 + 1) * 512], start=False, stop=True)
                    p.tt(c.t0[:, g2 * 512:(g2 + 1) * 512], ps_w[:, :], c.t0[:, g2 * 512:(g2 + 1) * 512], "add")
                p.dma(xout[bi, t * 128:(t + 1) * 128, :], c.t0[:, :])

def _row_spans(r0, n):
    out = []
    off = 0
    while n > 0:
        t = r0 // 128
        pp = r0 % 128
        cnt = min(n, 128 - pp)
        out.append((t, pp, cnt, off))
        r0 += cnt
        off += cnt
        n -= cnt
    return out


def emit_l0attna(p, c, xin, xout, Wd, tag):
    B = c.B
    NC = 32
    CK = 64
    Yhat = p.dram(tag + "Yhat", (B, NC, 66, 1024), F32)
    Htm = p.dram(tag + "Htm", (B, 2, 1024, 1024), F32)
    gd = p.dram(tag + "gd", (B, NC, 1024), F32)
    xld = p.dram(tag + "xld", (B, 2, 128, 1024), F32)
    alfd = p.dram(tag + "alfd", (B, 2, 128, 1024), F32)

    with p.scope():
        dftc = p.sb("dftc", (64, 66), F32)
        p.dma(dftc[:, :], Wd["c_dft"][:, :])
        pwrT = loadw_bf(p, c, "pwrT", Wd["a0_pwrT"], 8, 1024)
        pwiT = loadw_bf(p, c, "pwiT", Wd["a0_pwiT"], 8, 1024)
        gwT = loadw_bf(p, c, "gwT", Wd["a0_gwT"], 8, 1024)
        dwr = p.sb("dwr", (128, 8, 7), F32)
        dwi = p.sb("dwi", (128, 8, 7), F32)
        p.dma(dwr[:, :, :], Wd["a0_dwr"].rearrange("(g pp) kk -> pp g kk", pp=128))
        p.dma(dwi[:, :, :], Wd["a0_dwi"].rearrange("(g pp) kk -> pp g kk", pp=128))
        ps_d = p.ps("ps_d", (128, 512), F32)
        XF = p.sb("XF", (66, 1024), F32)
        xlr = p.sb("xlr", (128, 1024), F32)
        xli = p.sb("xli", (128, 1024), F32)
        xh_r = p.sb("xh_r", (128, 8, NC, 29), F32)
        xh_i = p.sb("xh_i", (128, 8, NC, 29), F32)
        cacc = p.sb("cacc", (128, NC, 29), F32)
        hcb = p.sb("hcb", (128, 8, NC * 29), BF16)
        gm = p.sb("gm", (32, 1024), F32)
        gmb = p.sb("gmb", (32, 1024), BF16)
        gmT = p.sb("gmT", (128, 8, 32), BF16)
        for bi in range(B):
            for t in range(NT):
                p.dma(c.t0[:, :], xin[bi, t * 128:(t + 1) * 128, :])
                ln_tile(p, c, c.t0[:, :], c.t0[:, :])
                for half in range(2):
                    ch = 2 * t + half
                    p.copy(c.t1[0:64, :], c.t0[half * 64:(half + 1) * 64, :], q="vector")
                    for g2 in range(2):
                        p.mm(ps_d[0:66, :], dftc[:, :], c.t1[0:64, g2 * 512:(g2 + 1) * 512], start=True, stop=True)
                        p.copy(XF[:, g2 * 512:(g2 + 1) * 512], ps_d[0:66, :])
                    p.dma(Yhat[bi, ch], XF[:, :])
                    p.dma(xlr[ch * 4:(ch + 1) * 4, :], XF[0:4, :])
                    p.dma(xli[ch * 4:(ch + 1) * 4, :], XF[33:37, :])
                    for g in range(8):
                        p.trans(c.pst_f[0:128, 0:66], XF[:, g * 128:(g + 1) * 128], c.ident[0:66, 0:66])
                        p.copy(xh_r[:, g, ch, :], c.pst_f[:, 4:33], q="vector")
                        p.copy(xh_i[:, g, ch, :], c.pst_f[:, 37:66], q="vector")
            p.dma(xld[bi, 0], xlr[:, :])
            p.dma(xld[bi, 1], xli[:, :])
            for ri, (xh, dw) in enumerate([(xh_r, dwr), (xh_i, dwi)]):
                for g in range(8):
                    p.ts(cacc[:, :, :], xh[:, g, :, :], dw[:, g, 6:7], "mult")
                    for j in range(6):
                        s = 6 - j
                        p.stt(cacc[:, :, s:29], xh[:, g, :, 0:29 - s], dw[:, g, j:j + 1], cacc[:, :, s:29], "mult", "add")
                    p.act(cacc[:, :, :], cacc[:, :, :], AF.Gelu)
                    p.copy(hcb[:, g, :], cacc.rearrange("pp a b -> pp (a b)"), q="vector")
                pwT = pwrT if ri == 0 else pwiT
                for mt in range(8):
                    rows = min(128, NC * 29 - mt * 128)
                    if rows <= 0:
                        break
                    for g2 in range(2):
                        for k in range(8):
                            p.mm(ps_d[0:rows, :], hcb[:, k, mt * 128:mt * 128 + rows], pwT[:, k, g2 * 512:(g2 + 1) * 512], start=(k == 0), stop=(k == 7))
                        p.copy(c.t1[0:rows, g2 * 512:(g2 + 1) * 512], ps_d[0:rows, :])
                    p.dma(Htm[bi, ri, mt * 128:mt * 128 + rows, :], c.t1[0:rows, :])
            p.dma(gm[0:32, :].rearrange("cc (o dd) -> cc o dd", o=1), xld[bi, 0].rearrange("(cc ff) dd -> cc ff dd", ff=4)[:, 0:1, :])
            p.act(gmb[:, :], gm[:, :], AF.Identity, scale=1.0 / 64.0)
            for g in range(8):
                tr128(p, c, gmT[:, g, :], gmb[:, g * 128:(g + 1) * 128], 32, 128)
            for g2 in range(2):
                for k in range(8):
                    p.mm(ps_d[0:32, :], gmT[:, k, :], gwT[:, k, g2 * 512:(g2 + 1) * 512], start=(k == 0), stop=(k == 7))
                p.act(gm[:, g2 * 512:(g2 + 1) * 512], ps_d[0:32, :], AF.Sigmoid)
            p.dma(gd[bi], gm[:, :])

    with p.scope():
        qwT = loadw_bf(p, c, "qwT", Wd["a0_qwT"], 8, 1024)
        kwT = loadw_bf(p, c, "kwT", Wd["a0_kwT"], 8, 256)
        vwT = loadw_bf(p, c, "vwT", Wd["a0_vwT"], 8, 256)
        owT = loadw_bf(p, c, "owT", Wd["a0_owT"], 8, 1024)
        bmask = p.sb("bmask", (128, 128), F32)
        p.dma(bmask[:, :], Wd["c_bmask"][:, :])
        acol = scalar_col(p, c, "acol", Wd["a0_alpha"], fn=lambda s: p.act(s[:, :], s[:, :], AF.Sigmoid), slot=3)
        xf = p.sb("xf0", (128, 1024), F32)
        xfb = p.sb("xfb0", (128, 1024), BF16)
        xfT = p.sb("xfT0", (128, 8, 128), BF16)
        q_sb = p.sb("q_sb", (128, 1024), BF16)
        kT_hs = p.sb("kT_hs", (64, 4, 128), BF16)
        v_sb = p.sb("v_sb", (128, 256), BF16)
        qTh = p.sb("qTh", (64, 128), BF16)
        aT = p.sb("aT0", (128, 8, 128), BF16)
        sc128 = p.sb("sc128", (128, 128), F32)
        scb128 = p.sb("scb128", (128, 128), BF16)
        ps_a = p.ps("ps_a", (128, 512), F32)
        ps_s = p.ps("ps_s0", (128, 128), F32)
        for bi in range(B):
            for ri in range(2):
                p.dma(xf[:, :], xld[bi, ri])
                p.copy(xfb[:, :], xf[:, :], q="vector")
                for g in range(8):
                    tr128(p, c, xfT[:, g, :], xfb[:, g * 128:(g + 1) * 128], 128, 128)
                for g2 in range(2):
                    for k in range(8):
                        p.mm(ps_a[:, :], xfT[:, k, :], qwT[:, k, g2 * 512:(g2 + 1) * 512], start=(k == 0), stop=(k == 7))
                    p.copy(q_sb[:, g2 * 512:(g2 + 1) * 512], ps_a[:, :], q="vector")
                for kv in range(4):
                    for k in range(8):
                        p.mm(ps_s[0:64, :], kwT[:, k, kv * 64:(kv + 1) * 64], xfT[:, k, :], start=(k == 0), stop=(k == 7))
                    p.copy(kT_hs[:, kv, :], ps_s[0:64, :], q="vector")
                for k in range(8):
                    p.mm(ps_a[:, 0:256], xfT[:, k, :], vwT[:, k, :], start=(k == 0), stop=(k == 7))
                p.copy(v_sb[:, :], ps_a[:, 0:256], q="vector")
                for h in range(16):
                    kv = h // 4
                    tr128(p, c, qTh[:, :], q_sb[:, h * 64:(h + 1) * 64], 128, 64)
                    p.mm(ps_s[:, :], qTh[:, :], kT_hs[:, kv, :], start=True, stop=True)
                    p.act(sc128[:, :], ps_s[:, :], AF.Identity, scale=0.125)
                    p.tt(sc128[:, :], sc128[:, :], bmask[:, :], "add")
                    p.red(c.c0[:, :], sc128[:, :], "max", negate=True)
                    p.act(sc128[:, :], sc128[:, :], AF.Exp, bias=c.c0[:, :])
                    p.red(c.c1[:, :], sc128[:, :], "add")
                    p.op("vector", lambda e: e.reciprocal(c.c1[:, :], c.c1[:, :]))
                    p.ts(sc128[:, :], sc128[:, :], c.c1[:, :], "mult")
                    p.copy(scb128[:, :], sc128[:, :], q="vector")
                    tr128(p, c, scb128[:, :], scb128[:, :], 128, 128)
                    p.mm(ps_s[0:64, :], v_sb[:, kv * 64:(kv + 1) * 64], scb128[:, :], start=True, stop=True)
                    goff = (h % 2) * 64
                    p.copy(aT[goff:goff + 64, h // 2, :], ps_s[0:64, :], q="vector")
                for g2 in range(2):
                    for k in range(8):
                        p.mm(ps_a[:, :], aT[:, k, :], owT[:, k, g2 * 512:(g2 + 1) * 512], start=(k == 0), stop=(k == 7))
                    p.copy(c.t1[:, g2 * 512:(g2 + 1) * 512], ps_a[:, :])
                p.tt(c.t1[:, :], c.t1[:, :], xf[:, :], "subtract")
                p.stt(c.t1[:, :], c.t1[:, :], acol[:, :], xf[:, :], "mult", "add")
                p.dma(alfd[bi, ri], c.t1[:, :])

    with p.scope():
        idftc = p.sb("idftc", (66, 64), F32)
        p.dma(idftc[:, :], Wd["c_idft"][:, :])
        g_exp = p.sb("g_exp", (128, 8, 1024), F32)
        Ah = p.sb("Ah", (128, 8, 1024), F32)
        Hx = p.sb("Hx", (128, 8, 1024), F32)
        yf = p.sb("yf", (66, 1024), F32)
        o64 = p.sb("o64", (64, 1024), F32)
        ps_i = p.ps("ps_i", (128, 512), F32)
        Ahd = p.dram(tag + "Ahd", (2, 1024, 1024), F32)
        for bi in range(B):
            for ch in range(NC):
                for (tt_, pp_, cnt_, off_) in _row_spans(ch * 29, 29):
                    p.dma(g_exp[pp_:pp_ + cnt_, tt_, :], gd[bi, ch:ch + 1, :].broadcast_to([cnt_, 1024]))
            for ri in range(2):
                for g in range(8):
                    p.dma(Hx[:, g, :], Htm[bi, ri, g * 128:(g + 1) * 128, :])
                base = 4 if ri == 0 else 37
                for ch in range(NC):
                    for (tt_, pp_, cnt_, off_) in _row_spans(ch * 29, 29):
                        p.dma(Ah[pp_:pp_ + cnt_, tt_, :], Yhat[bi, ch, base + off_:base + off_ + cnt_, :])
                for g in range(8):
                    p.tt(Hx[:, g, :], Hx[:, g, :], g_exp[:, g, :], "mult")
                    p.tt(Ah[:, g, :], Ah[:, g, :], Hx[:, g, :], "add")
                for g in range(8):
                    p.dma(Ahd[ri, g * 128:(g + 1) * 128, :], Ah[:, g, :])
            for ch in range(NC):
                p.dma(yf[0:4, :], alfd[bi, 0][ch * 4:(ch + 1) * 4, :])
                p.dma(yf[33:37, :], alfd[bi, 1][ch * 4:(ch + 1) * 4, :])
                p.dma(yf[4:33, :], Ahd[0, ch * 29:(ch + 1) * 29, :])
                p.dma(yf[37:66, :], Ahd[1, ch * 29:(ch + 1) * 29, :])
                for g2 in range(2):
                    p.mm(ps_i[0:64, :], idftc[:, :], yf[:, g2 * 512:(g2 + 1) * 512], start=True, stop=True)
                    p.copy(o64[:, g2 * 512:(g2 + 1) * 512], ps_i[0:64, :])
                p.dma(c.t0[0:64, :], xin[bi, ch * 64:(ch + 1) * 64, :])
                p.tt(o64[:, :], o64[:, :], c.t0[0:64, :], "add")
                p.dma(xout[bi, ch * 64:(ch + 1) * 64, :], o64[:, :])


SUBLAYERS = [
    ("a0", emit_l0attna), ("f0a", emit_l0ffn), ("b0", emit_l0attnb), ("f0b", emit_l0ffn),
    ("a1", emit_l1attna), ("f1a", emit_l1ffn), ("b1", emit_l1attnb), ("f1b", emit_l1ffn),
]
PREFIX = {"a0": "a0_", "f0a": "f0_", "b0": "b0_", "f0b": "f0_",
          "a1": "a1_", "f1a": "f1_", "b1": "b1_", "f1b": "f1_"}


def build_single(name, emit, Wnp, B, xin_np):
    p = P()
    with p.scope():
        c = Ctx(p, B)
        Wd = {}
        pref = PREFIX[name]
        for k, v in Wnp.items():
            if k.startswith(pref) or k.startswith("c_"):
                Wd[k] = p.inp(k, v)
        xin = p.inp("x_in", xin_np)
        xout = p.out("x_out", (B, SEQ, D), F32)
        ctx_init(p, c, Wd)
        emit(p, c, xin, xout, Wd, name)
    return p


NCORES = 8
BLOC = 4

CHAIN = [("a0", emit_l0attna), ("f0", emit_l0ffn), ("b0", emit_l0attnb), ("f0", emit_l0ffn),
         ("a1", emit_l1attna), ("f1", emit_l1ffn), ("b1", emit_l1attnb), ("f1", emit_l1ffn)]


def _build_fused(Wnp):
    p = P()
    with p.scope():
        c = Ctx(p, BLOC)
        Wd = {}
        wkeys = []
        for k in sorted(Wnp.keys()):
            Wd[k] = p.inp(k, Wnp[k])
            wkeys.append(k)
        xin = p.inp("x_in", np.zeros((BLOC, SEQ, D), np.float32))
        xout = p.out("x_out", (BLOC, SEQ, D), F32)
        ctx_init(p, c, Wd)
        xa = p.dram("xa_ping", (BLOC, SEQ, D), F32)
        xb = p.dram("xb_pong", (BLOC, SEQ, D), F32)
        cur = xin
        for i, (name, emit) in enumerate(CHAIN):
            nxt = xout if i == len(CHAIN) - 1 else (xa if i % 2 == 0 else xb)
            with p.scope():
                emit(p, c, cur, nxt, Wd, name + str(i))
            cur = nxt
    nc = p.finalize()
    return nc, wkeys, p


_cache = {}


def kernel(**inputs):
    Wnp = prep_weights(inputs)
    if "prog" not in _cache:
        _cache["prog"] = _build_fused(Wnp)
    nc, wkeys, _p = _cache["prog"]
    x = np.ascontiguousarray(np.asarray(inputs["L0_attn_a_hidden_states"], np.float32))
    assert x.shape[0] == NCORES * BLOC
    wmap = {k: Wnp[k] for k in wkeys}
    in_maps = [dict(wmap, x_in=np.ascontiguousarray(x[ci * BLOC:(ci + 1) * BLOC]))
               for ci in range(NCORES)]
    res = run_bass_kernel_spmd(nc, in_maps, core_ids=list(range(NCORES)))
    return np.concatenate([np.asarray(res.results[ci]["x_out"], np.float32) for ci in range(NCORES)], axis=0)


# version 03 (final)
# speedup vs baseline: 1.3879x; 1.3879x over previous
import numpy as np
from contextlib import ExitStack
import concourse.bass as bass
import concourse.mybir as mybir
from concourse.bass_utils import run_bass_kernel_spmd

F32 = mybir.dt.float32
BF16 = mybir.dt.bfloat16
AF = mybir.ActivationFunctionType
ALU = mybir.AluOpType
AX = mybir.AxisListType

_DT = {np.dtype(np.float32): F32, np.dtype(np.int32): mybir.dt.int32}


def np_to_dt(d):
    return _DT[np.dtype(d)]


class P:

    def __init__(self, name="prog"):
        self.nc = bass.Bass("TRN2", target_bir_lowering=False)
        self.stack = ExitStack()
        self.sem = self.stack.enter_context(self.nc.semaphore("gsem"))
        self.cnt = 0
        self.qs = ("sync", "tensor", "vector", "scalar", "gpsimd")
        self.ops = {q: [] for q in self.qs}
        self.io = {}
        self._grp = None
        self._scopes = []
        self._uid = 0

    def uname(self, base):
        self._uid += 1
        return f"{base}_{self._uid}"

    def scope(self):
        outer = self

        class _S:
            def __enter__(self):
                outer._scopes.append(ExitStack())

            def __exit__(self, *a):
                outer._scopes.pop().close()

        return _S()

    def dram(self, name, shape, dtype=F32, kind="Internal"):
        return self.nc.dram_tensor(name, list(shape), dtype, kind=kind).ap()

    def inp(self, name, arr):
        arr = np.ascontiguousarray(arr)
        self.io[name] = arr
        return self.dram(name, arr.shape, np_to_dt(arr.dtype), kind="ExternalInput")

    def out(self, name, shape, dtype=F32):
        return self.dram(name, shape, dtype, kind="ExternalOutput")

    def sb(self, name, shape, dtype=F32):
        st = self._scopes[-1] if self._scopes else self.stack
        return st.enter_context(self.nc.sbuf_tensor(self.uname(name), list(shape), dtype))

    def ps(self, name, shape, dtype=F32):
        st = self._scopes[-1] if self._scopes else self.stack
        return st.enter_context(self.nc.psum_tensor(self.uname(name), list(shape), dtype))

    def _emit(self, q, fn, inc):
        if self._grp is None:
            self.ops[q].append((self.cnt, fn, inc))
            self.cnt += inc
        else:
            self.ops[q].append((self._grp[0], fn, inc))
            self._grp[1] += inc

    def group(self):
        outer = self

        class _G:
            def __enter__(self):
                outer._grp = [outer.cnt, 0]

            def __exit__(self, *a):
                outer.cnt += outer._grp[1]
                outer._grp = None

        return _G()

    def dma(self, dst, src, q="sync", **kw):
        self._emit(q, lambda e, d=dst, s=src: e.dma_start(d, s, **kw), 16)

    def op(self, q, fn):
        self._emit(q, fn, 1)

    def mm(self, out, lhsT, rhs, start=True, stop=True):
        self._emit("tensor", lambda e: e.matmul(out, lhsT, rhs, start=start, stop=stop), 1)

    def trans(self, out, in_, ident):
        self._emit("tensor", lambda e: e.transpose(out, in_, ident), 1)

    def act(self, out, in_, func, bias=..., scale=...):
        kw = {}
        if bias is not ...:
            kw["bias"] = bias
        if scale is not ...:
            kw["scale"] = scale
        self._emit("scalar", lambda e: e.activation(out, in_, func, **kw), 1)

    def copy(self, out, in_, q="scalar"):
        if q == "scalar":
            self._emit(q, lambda e: e.copy(out, in_), 1)
        else:
            self._emit(q, lambda e: e.tensor_copy(out, in_), 1)

    def tt(self, out, in0, in1, opname, q="vector"):
        o = getattr(ALU, opname)
        self._emit(q, lambda e: e.tensor_tensor(out, in0, in1, o), 1)

    def ts(self, out, in0, scalar, opname, q="vector"):
        o = getattr(ALU, opname)
        self._emit(q, lambda e: e.tensor_scalar(out, in0, scalar, None, o), 1)

    def stt(self, out, in0, scalar, in1, op0, op1, q="vector"):
        o0, o1 = getattr(ALU, op0), getattr(ALU, op1)
        self._emit(q, lambda e: e.scalar_tensor_tensor(out, in0, scalar, in1, o0, o1), 1)

    def red(self, out, in_, opname, negate=None, q="vector"):
        o = getattr(ALU, opname)
        self._emit(q, lambda e: e.tensor_reduce(out, in_, AX.X, o, negate=negate), 1)

    def memset(self, ap, v, q="vector"):
        self._emit(q, lambda e: e.memset(ap, v), 1)

    def iota(self, ap, pattern, base=0, channel_multiplier=0):
        self._emit(
            "gpsimd",
            lambda e: e.iota(ap, pattern, base=base, channel_multiplier=channel_multiplier,
                             allow_small_or_imprecise_dtypes=True),
            1,
        )

    def finalize(self):
        nc = self.nc
        total = self.cnt
        sem = self.sem

        def runner(eng, lst, final=False):
            for (val, fn, inc) in lst:
                if val > 0:
                    eng.wait_ge(sem, val)
                inst = fn(eng)
                inst.then_inc(sem, inc)
            if final:
                eng.wait_ge(sem, total)

        with nc.Block() as block:
            if self.ops["sync"] or True:
                @block.sync
                def _(eng):
                    runner(eng, self.ops["sync"], final=True)
            if self.ops["tensor"]:
                @block.tensor
                def _(eng):
                    runner(eng, self.ops["tensor"])
            if self.ops["vector"]:
                @block.vector
                def _(eng):
                    runner(eng, self.ops["vector"])
            if self.ops["scalar"]:
                @block.scalar
                def _(eng):
                    runner(eng, self.ops["scalar"])
            if self.ops["gpsimd"]:
                @block.gpsimd
                def _(eng):
                    runner(eng, self.ops["gpsimd"])
        return nc

    def run(self, in_maps=None, n_cores=1, **kw):
        nc = self.finalize()
        if in_maps is None:
            in_maps = [dict(self.io) for _ in range(n_cores)]
        res = run_bass_kernel_spmd(nc, in_maps, core_ids=list(range(n_cores)), **kw)
        self.stack.close()
        return res



D = 1024
SEQ = 2048
NT = SEQ // 128


def host_consts():
    C = {}
    C["c_ident"] = np.eye(128, dtype=np.float32)
    p_ = np.arange(64)[:, None]
    f_ = np.arange(33)[None, :]
    ang = 2 * np.pi * p_ * f_ / 64.0
    dftc = np.concatenate([np.cos(ang), -np.sin(ang)], axis=1).astype(np.float32)
    C["c_dft"] = dftc
    t_ = np.arange(64)[None, :]
    f2 = np.arange(33)[:, None]
    w = np.where((f2 == 0) | (f2 == 32), 1.0, 2.0)
    ang2 = 2 * np.pi * f2 * t_ / 64.0
    idft = np.concatenate([w * np.cos(ang2) / 64.0, -w * np.sin(ang2) / 64.0], axis=0).astype(np.float32)
    C["c_idft"] = idft
    xr = np.random.RandomState(1).randn(5, 64).astype(np.float64)
    X = np.fft.rfft(xr, axis=1)
    Xm = xr @ dftc.astype(np.float64)
    assert np.allclose(Xm[:, :33], X.real, atol=1e-6) and np.allclose(Xm[:, 33:], X.imag, atol=1e-6)
    yb = np.concatenate([X.real, X.imag], 1) @ idft.astype(np.float64)
    assert np.allclose(yb, xr, atol=1e-9)
    i_ = np.arange(128)
    m = (i_[:, None] // 4 >= i_[None, :] // 4)
    C["c_bmask"] = np.where(m, 0.0, -1e30).astype(np.float32)
    m2 = (i_[:, None] >= i_[None, :])
    C["c_tmask"] = np.where(m2, 0.0, -1e30).astype(np.float32)
    return C


def prep_weights(inp):
    T = lambda a: np.ascontiguousarray(np.asarray(a, np.float32).T)
    A = lambda a: np.ascontiguousarray(np.asarray(a, np.float32))
    W = {}
    W["a0_qwT"] = T(inp["L0_attn_a_q_w"])
    W["a0_kwT"] = T(inp["L0_attn_a_k_w"])
    W["a0_vwT"] = T(inp["L0_attn_a_v_w"])
    W["a0_owT"] = T(inp["L0_attn_a_o_w"])
    W["a0_dwr"] = A(inp["L0_attn_a_dw_r"]).reshape(1024, 7)
    W["a0_dwi"] = A(inp["L0_attn_a_dw_i"]).reshape(1024, 7)
    W["a0_pwrT"] = A(inp["L0_attn_a_pw_r"])
    W["a0_pwrT"] = T(inp["L0_attn_a_pw_r"])
    W["a0_pwiT"] = T(inp["L0_attn_a_pw_i"])
    W["a0_gwT"] = T(inp["L0_attn_a_gate_w"])
    W["a0_alpha"] = A(inp["L0_attn_a_alpha_raw"]).reshape(1, 1)
    W["b0_logdt"] = A(inp["L0_attn_b_log_dt"]).reshape(1, 1024)
    W["b0_lwr"] = A(inp["L0_attn_b_log_w_real"]).reshape(1, 32)
    W["b0_wim"] = A(inp["L0_attn_b_w_imag"]).reshape(1, 32)
    W["b0_Cr"] = A(inp["L0_attn_b_C_real"]).reshape(1024, 32)
    W["b0_Ci"] = A(inp["L0_attn_b_C_imag"]).reshape(1024, 32)
    W["b0_D"] = A(inp["L0_attn_b_D"]).reshape(1, 1024)
    W["b0_WoT"] = T(inp["L0_attn_b_W_out"])
    W["b0_bo"] = A(inp["L0_attn_b_b_out"]).reshape(1, 1024)
    W["f0_sinit"] = A(inp["L0_ffn_slot_init"])
    W["f0_sscale"] = A(inp["L0_ffn_slot_scale"])
    w1 = A(inp["L0_ffn_w1"])
    W["f0_w1aT"] = T(w1[:, :1024])
    W["f0_w1bT"] = T(w1[:, 1024:])
    W["f0_b1"] = A(inp["L0_ffn_b1"]).reshape(1, 512)
    W["f0_w2T"] = T(inp["L0_ffn_w2"])
    W["f0_b2"] = A(inp["L0_ffn_b2"]).reshape(1, 64)
    W["f0_wihT"] = T(inp["L0_ffn_gru_wih"])
    W["f0_whhT"] = T(inp["L0_ffn_gru_whh"])
    W["f0_bih"] = A(inp["L0_ffn_gru_bih"]).reshape(1, 3072)
    W["f0_bhh"] = A(inp["L0_ffn_gru_bhh"]).reshape(1, 3072)
    W["f0_wvT"] = T(inp["L0_ffn_wv"])
    W["f0_bv"] = A(inp["L0_ffn_bv"]).reshape(1, 1024)
    W["f0_woT"] = T(inp["L0_ffn_wo"])
    W["f0_bo"] = A(inp["L0_ffn_bo"]).reshape(1, 1024)
    W["f0_tau"] = A(inp["L0_ffn_tau"]).reshape(1, 1)
    W["a1_nw"] = A(inp["L1_attn_a_norm_w"]).reshape(1, 1024)
    W["a1_Wq"] = A(inp["L1_attn_a_Wq"])
    W["a1_Wk"] = A(inp["L1_attn_a_Wk"])
    W["a1_Wv"] = A(inp["L1_attn_a_Wv"])
    W["a1_Wo"] = A(inp["L1_attn_a_Wo"])
    W["a1_gate"] = A(inp["L1_attn_a_gate"]).reshape(1, 1)
    W["a1_ltemp"] = A(inp["L1_attn_a_log_temperature"]).reshape(1, 1)
    W["b1_Wq"] = A(inp["L1_attn_b_Wq"])
    W["b1_bq"] = A(inp["L1_attn_b_bq"]).reshape(1024)
    W["b1_Wl"] = A(inp["L1_attn_b_Wl"])
    W["b1_bl"] = A(inp["L1_attn_b_bl"]).reshape(256)
    W["b1_Wk"] = A(inp["L1_attn_b_Wk"])
    W["b1_bk"] = A(inp["L1_attn_b_bk"]).reshape(1024)
    W["b1_Wv"] = A(inp["L1_attn_b_Wv"])
    W["b1_bv"] = A(inp["L1_attn_b_bv"]).reshape(1, 1024)
    W["b1_Wo"] = A(inp["L1_attn_b_Wo"])
    W["b1_bo"] = A(inp["L1_attn_b_bo"]).reshape(1, 1024)
    W["b1_bq_pg"] = A(W["b1_bq"]).reshape(8, 128).T.copy()
    W["b1_bl_pg"] = A(W["b1_bl"]).reshape(2, 128).T.copy()
    W["b1_bk_pg"] = A(W["b1_bk"]).reshape(8, 128).T.copy()
    W["b1_Wo_hs"] = np.ascontiguousarray(A(inp["L1_attn_b_Wo"]).reshape(16, 64, 1024).transpose(1, 0, 2))
    W["a1_nw_pg"] = A(inp["L1_attn_a_norm_w"]).reshape(8, 128).T.copy()
    W["b0_logdt_pg"] = A(inp["L0_attn_b_log_dt"]).reshape(8, 128).T.copy()
    W["b0_D_pg"] = A(inp["L0_attn_b_D"]).reshape(8, 128).T.copy()
    W["f1_rwT"] = T(inp["L1_ffn_router_w"])
    W["f1_w1T"] = np.ascontiguousarray(np.asarray(inp["L1_ffn_w1"], np.float32).transpose(0, 2, 1))
    W["f1_w2T"] = np.ascontiguousarray(np.asarray(inp["L1_ffn_w2"], np.float32).transpose(0, 2, 1))
    W.update(host_consts())
    return {k: np.ascontiguousarray(v, dtype=np.float32) for k, v in W.items()}


class Ctx:
    def __init__(self, p, B):
        self.B = B
        self.ident = p.sb("ident", (128, 128), F32)
        self.identb = p.sb("identb", (128, 128), BF16)
        self.ones1b = p.sb("ones1b", (1, 128), BF16)
        self.onescolb = p.sb("onescolb", (128, 1), BF16)
        self.pst_b = p.ps("pst_b", (128, 128), BF16)
        self.pst_f = p.ps("pst_f", (128, 128), F32)
        self.t0 = p.sb("ct0", (128, 1024), F32)
        self.t1 = p.sb("ct1", (128, 1024), F32)
        self.c0 = p.sb("cc0", (128, 1), F32)
        self.c1 = p.sb("cc1", (128, 1), F32)
        self.bcast = p.dram("cbcast", (1, 16), F32)


def ctx_init(p, c, Wd):
    p.dma(c.ident[:, :], Wd["c_ident"][:, :])
    p.copy(c.identb[:, :], c.ident[:, :], q="vector")
    p.memset(c.ones1b[:, :], 1.0)
    p.memset(c.onescolb[:, :], 1.0)


def ln_tile(p, c, dst, src, eps=1e-5):
    p.red(c.c0[:, :], src, "add")
    p.ts(c.c0[:, :], c.c0[:, :], 1.0 / D, "mult")
    p.ts(dst, src, c.c0[:, :], "subtract")
    p.act(c.t1[:, :], dst, AF.Square)
    p.red(c.c1[:, :], c.t1[:, :], "add")
    p.ts(c.c1[:, :], c.c1[:, :], 1.0 / D, "mult")
    p.ts(c.c1[:, :], c.c1[:, :], eps, "add")
    p.act(c.c1[:, :], c.c1[:, :], AF.Ln)
    p.ts(c.c1[:, :], c.c1[:, :], -0.5, "mult")
    p.act(c.c1[:, :], c.c1[:, :], AF.Exp)
    p.ts(dst, dst, c.c1[:, :], "mult")


def tr128(p, c, dst, src, r, k, fp32=False):
    ps = c.pst_f if fp32 else c.pst_b
    idn = c.ident if fp32 else c.identb
    p.trans(ps[0:k, 0:r], src, idn[0:r, 0:r])
    p.copy(dst, ps[0:k, 0:r], q="vector")


def loadw_bf(p, c, name, dram, kt, n):
    b = p.sb(name, (128, kt, n), BF16)
    for k in range(kt):
        for c0 in range(0, n, 1024):
            cw = min(1024, n - c0)
            p.dma(c.t0[:, 0:cw], dram[k * 128:(k + 1) * 128, c0:c0 + cw])
            p.copy(b[:, k, c0:c0 + cw], c.t0[:, 0:cw], q="vector")
    return b


def loadrow_bf(p, name, dram, n):
    f = p.sb(name + "f", (1, n), F32)
    b = p.sb(name, (1, n), BF16)
    p.dma(f[:, :], dram[:, :])
    p.copy(b[:, :], f[:, :], q="vector")
    return b, f


def scalar_col(p, c, name, dram11, fn=None, slot=0):
    s = p.sb(name + "s", (1, 1), F32)
    col = p.sb(name, (128, 1), F32)
    p.dma(s[:, :], dram11[:, :])
    if fn is not None:
        fn(s)
    p.dma(c.bcast[0:1, slot:slot + 1], s[:, :])
    p.dma(col[:, :], c.bcast[0:1, slot:slot + 1].broadcast_to([128, 1]))
    return col


def lnT_tiles(p, c, xin_b, lnT, lnb_all=None, rms=False, eps=1e-5):
    for t in range(NT):
        p.dma(c.t0[:, :], xin_b[t * 128:(t + 1) * 128, :])
        ln_tile(p, c, c.t0[:, :], c.t0[:, :], eps=eps)
        lb = p.sb("lnb", (128, 1024), BF16) if lnb_all is None else None
        tgt = lnb_all[:, t, :] if lnb_all is not None else lb[:, :]
        p.copy(tgt, c.t0[:, :], q="vector")
        for g in range(8):
            tr128(p, c, lnT[:, g, t * 128:(t + 1) * 128], tgt[:, g * 128:(g + 1) * 128] if lnb_all is not None else lb[:, g * 128:(g + 1) * 128], 128, 128)


def emit_l1ffn(p, c, xin, xout, Wd, tag):
    B = c.B
    ntiles = B * NT
    w1bf = p.dram(tag + "w1bf", (8, 1024, 4096), BF16)
    w2bf = p.dram(tag + "w2bf", (8, 4096, 1024), BF16)
    with p.scope():
        f = p.sb("wcast", (128, 4096), F32)
        b = p.sb("wcastb", (128, 4096), BF16)
        for e in range(8):
            for k in range(8):
                p.dma(f[:, :], Wd["f1_w1T"][e, k * 128:(k + 1) * 128, :])
                p.copy(b[:, :], f[:, :], q="vector")
                p.dma(w1bf[e, k * 128:(k + 1) * 128, :], b[:, :])
            for k in range(32):
                p.dma(f[:, 0:1024], Wd["f1_w2T"][e, k * 128:(k + 1) * 128, :])
                p.copy(b[:, 0:1024], f[:, 0:1024], q="vector")
                p.dma(w2bf[e, k * 128:(k + 1) * 128, :], b[:, 0:1024])

    xTd = p.dram(tag + "xT", (ntiles, 8, 128, 128), BF16)
    accd = p.dram(tag + "acc", (ntiles, 128, 1024), F32)
    wts = p.sb(tag + "wts", (128, ntiles, 8), F32)

    with p.scope():
        rwT = loadw_bf(p, c, "rwT", Wd["f1_rwT"], 8, 8)
        xT = p.sb("xTt", (128, 8, 128), BF16)
        lnb = p.sb("lnbt", (128, 1024), BF16)
        ps_r = p.ps("ps_r", (128, 8), F32)
        lg = p.sb("lg", (128, 8), F32)
        wk = p.sb("wkk", (128, 8), F32)
        kp = p.sb("kpp", (128, 8), F32)
        for bi in range(B):
            for t in range(NT):
                ti = bi * NT + t
                p.dma(c.t0[:, :], xin[bi, t * 128:(t + 1) * 128, :])
                ln_tile(p, c, c.t0[:, :], c.t0[:, :])
                p.copy(lnb[:, :], c.t0[:, :], q="vector")
                for g in range(8):
                    tr128(p, c, xT[:, g, :], lnb[:, g * 128:(g + 1) * 128], 128, 128)
                p.dma(xTd[ti].rearrange("k pp cc -> pp k cc"), xT[:, :, :])
                for k in range(8):
                    p.mm(ps_r[:, :], xT[:, k, :], rwT[:, k, :], start=(k == 0), stop=(k == 7))
                p.red(c.c0[:, :], ps_r[:, :], "max", negate=True)
                p.act(lg[:, :], ps_r[:, :], AF.Exp, bias=c.c0[:, :])
                p.red(c.c1[:, :], lg[:, :], "add")
                p.op("vector", lambda e: e.reciprocal(c.c1[:, :], c.c1[:, :]))
                p.ts(lg[:, :], lg[:, :], c.c1[:, :], "mult")
                p.copy(wk[:, :], lg[:, :], q="vector")
                p.red(c.c0[:, :], wk[:, :], "max")
                p.ts(kp[:, :], wk[:, :], c.c0[:, :], "is_ge")
                p.stt(wk[:, :], kp[:, :], -1e30, wk[:, :], "mult", "add")
                p.red(c.c0[:, :], wk[:, :], "max")
                p.ts(kp[:, :], lg[:, :], c.c0[:, :], "is_ge")
                p.tt(wts[:, ti, :], lg[:, :], kp[:, :], "mult")
        if getattr(p, "dbg_l1ffn", False):
            wo_ = p.out(tag + "_dbg_wts", (128, ntiles, 8), F32)
            p.dma(wo_, wts[:, :, :])

    if getattr(p, "dbg_l1ffn", False):
        xo_ = p.out(tag + "_dbg_xT", (ntiles, 8, 128, 128), BF16)
        with p.scope():
            tmpx = p.sb("tmpx", (128, 8, 128), BF16)
            for ti in range(ntiles):
                p.dma(tmpx[:, :, :], xTd[ti].rearrange("k pp cc -> pp k cc"))
                p.dma(xo_[ti].rearrange("k pp cc -> pp k cc"), tmpx[:, :, :])
    GRP = 4
    ngrp = ntiles // GRP
    with p.scope():
        w1h = p.sb("w1h", (128, 8, 4096), BF16)
        w2h = p.sb("w2h", (128, 32, 1024), BF16)
        xTh = p.sb("xTh", (128, 8, GRP * 128), BF16)
        hTb = p.sb("hTb", (128, 32, GRP * 128), BF16)
        at = p.sb("at", (128, 1024), F32)
        ps_h = p.ps("ps_h", (128, 512), F32)
        ps_y = p.ps("ps_y", (128, 512), F32)
        for e in range(8):
            with p.group():
                for k in range(8):
                    p.dma(w1h[:, k, :], w1bf[e, k * 128:(k + 1) * 128, :])
                for k in range(32):
                    p.dma(w2h[:, k, :], w2bf[e, k * 128:(k + 1) * 128, :])
            for gi in range(ngrp):
                with p.group():
                    for i in range(GRP):
                        p.dma(xTh[:, :, i * 128:(i + 1) * 128], xTd[gi * GRP + i].rearrange("k pp cc -> pp k cc"))
                for ht in range(32):
                    with p.group():
                        for k in range(8):
                            p.mm(ps_h[:, :], w1h[:, k, ht * 128:(ht + 1) * 128], xTh[:, k, :], start=(k == 0), stop=(k == 7))
                    p.act(hTb[:, ht, :], ps_h[:, :], AF.Gelu)
                for i in range(GRP):
                    ti = gi * GRP + i
                    if e == 0:
                        p.memset(at[:, :], 0.0)
                    else:
                        p.dma(at[:, :], accd[ti])
                    for g2 in range(2):
                        with p.group():
                            for k in range(32):
                                p.mm(ps_y[:, :], hTb[:, k, i * 128:(i + 1) * 128], w2h[:, k, g2 * 512:(g2 + 1) * 512], start=(k == 0), stop=(k == 31))
                        p.stt(at[:, g2 * 512:(g2 + 1) * 512], ps_y[:, :], wts[:, ti, e:e + 1], at[:, g2 * 512:(g2 + 1) * 512], "mult", "add")
                    p.dma(accd[ti], at[:, :])
    dbg_acc = p.out(tag + "_dbg_acc", (ntiles, 128, 1024), F32) if getattr(p, "dbg_l1ffn", False) else None
    with p.scope():
        at = p.sb("resa", (128, 1024), F32)
        for bi in range(B):
            for t in range(NT):
                ti = bi * NT + t
                p.dma(at[:, :], accd[ti])
                if dbg_acc is not None:
                    p.dma(dbg_acc[ti], at[:, :])
                p.dma(c.t0[:, :], xin[bi, t * 128:(t + 1) * 128, :])
                p.tt(at[:, :], at[:, :], c.t0[:, :], "add")
                p.dma(xout[bi, t * 128:(t + 1) * 128, :], at[:, :])


def emit_l1attnb(p, c, xin, xout, Wd, tag):
    B = c.B
    qTd = p.dram(tag + "qTd", (8, 128, SEQ), BF16)
    kTd = p.dram(tag + "kTd", (8, 128, SEQ), BF16)
    vd = p.dram(tag + "vd", (NT, 128, 1024), BF16)
    accd = p.dram(tag + "accd", (NT, 128, 1024), F32)
    with p.scope():
        wq = loadw_bf(p, c, "wq", Wd["b1_Wq"], 8, 1024)
        wl = loadw_bf(p, c, "wl", Wd["b1_Wl"], 8, 256)
        wk = loadw_bf(p, c, "wk", Wd["b1_Wk"], 2, 1024)
        wv = loadw_bf(p, c, "wv", Wd["b1_Wv"], 2, 1024)
        wo = p.sb("wohs", (64, 16, 1024), BF16)
        wof = p.sb("wohsf", (64, 1024), F32)
        for h in range(16):
            p.dma(wof[:, :], Wd["b1_Wo_hs"][:, h, :])
            p.copy(wo[:, h, :], wof[:, :], q="vector")
        bq = p.sb("bqc", (128, 8), F32)
        bl = p.sb("blc", (128, 2), F32)
        bk = p.sb("bkc", (128, 8), F32)
        p.dma(bq[:, :], Wd["b1_bq_pg"][:, :])
        p.dma(bl[:, :], Wd["b1_bl_pg"][:, :])
        p.dma(bk[:, :], Wd["b1_bk_pg"][:, :])
        bvb, _ = loadrow_bf(p, "bvr", Wd["b1_bv"], 1024)
        bo32 = p.sb("bo32", (128, 1024), F32)
        p.dma(bo32[:, :], Wd["b1_bo"].broadcast_to([128, 1024]))
        ps1 = p.ps("ps1", (128, 512), F32)
        ps_cs = p.ps("ps_cs", (1, 512), F32)
        ps_pv = p.ps("ps_pv", (64, 512), F32)
        ps_o = p.ps("ps_o", (128, 512), F32)
        ev = p.sb("evac", (128, 512), BF16)

        for bi in range(B):
            with p.scope():
                hT = p.sb("hT", (128, 8, SEQ), BF16)
                latT = p.sb("latT", (128, 2, SEQ), BF16)
                lnT_tiles(p, c, xin[bi], hT)
                for g in range(2):
                    for g2 in range(4):
                        for k in range(8):
                            p.mm(ps1[:, :], wl[:, k, g * 128:(g + 1) * 128], hT[:, k, g2 * 512:(g2 + 1) * 512], start=(k == 0), stop=(k == 7))
                        p.act(latT[:, g, g2 * 512:(g2 + 1) * 512], ps1[:, :], AF.Identity, bias=bl[:, g:g + 1])
                for g in range(8):
                    for g2 in range(4):
                        for k in range(8):
                            p.mm(ps1[:, :], wq[:, k, g * 128:(g + 1) * 128], hT[:, k, g2 * 512:(g2 + 1) * 512], start=(k == 0), stop=(k == 7))
                        p.act(ev[:, :], ps1[:, :], AF.Identity, bias=bq[:, g:g + 1])
                        p.dma(qTd[g, :, g2 * 512:(g2 + 1) * 512], ev[:, :])
                        for k in range(2):
                            p.mm(ps1[:, :], wk[:, k, g * 128:(g + 1) * 128], latT[:, k, g2 * 512:(g2 + 1) * 512], start=(k == 0), stop=(k == 1))
                        p.act(ev[:, :], ps1[:, :], AF.Identity, bias=bk[:, g:g + 1])
                        p.dma(kTd[g, :, g2 * 512:(g2 + 1) * 512], ev[:, :])
                for t in range(NT):
                    for g2 in range(2):
                        for k in range(2):
                            p.mm(ps1[:, :], latT[:, k, t * 128:(t + 1) * 128], wv[:, k, g2 * 512:(g2 + 1) * 512], start=(k == 0), stop=False)
                        p.mm(ps1[:, :], c.ones1b[:, :], bvb[:, g2 * 512:(g2 + 1) * 512], start=False, stop=True)
                        p.copy(ev[:, :], ps1[:, :], q="vector")
                        p.dma(vd[t, :, g2 * 512:(g2 + 1) * 512], ev[:, :])
            with p.scope():
                qTg = p.sb("qTg", (64, SEQ), BF16)
                kTg = p.sb("kTg", (64, SEQ), BF16)
                vh = p.sb("vh", (128, NT, 64), BF16)
                eST = p.sb("eST", (128, NT, SEQ), BF16)
                sums = p.sb("sums", (1, SEQ), F32)
                rcp = p.sb("rcp", (128, NT), F32)
                attnTh = p.sb("attnTh", (64, SEQ), BF16)
                at = p.sb("at", (128, 1024), F32)
                for h in range(16):
                    g = h // 2
                    off = (h % 2) * 64
                    p.dma(qTg[:, :], qTd[g, off:off + 64, :])
                    p.dma(kTg[:, :], kTd[g, off:off + 64, :])
                    for t in range(NT):
                        p.dma(vh[:, t, :], vd[t, :, h * 64:(h + 1) * 64])
                    for kt in range(NT):
                        for g2 in range(4):
                            p.mm(ps1[:, :], kTg[:, kt * 128:(kt + 1) * 128], qTg[:, g2 * 512:(g2 + 1) * 512], start=True, stop=True)
                            p.act(eST[:, kt, g2 * 512:(g2 + 1) * 512], ps1[:, :], AF.Exp, scale=0.125)
                    for g2 in range(4):
                        for kt in range(NT):
                            p.mm(ps_cs[:, :], c.onescolb[:, :], eST[:, kt, g2 * 512:(g2 + 1) * 512], start=(kt == 0), stop=(kt == NT - 1))
                        p.copy(sums[:, g2 * 512:(g2 + 1) * 512], ps_cs[:, :])
                    for t in range(NT):
                        tr128(p, c, rcp[:, t:t + 1], sums[:, t * 128:(t + 1) * 128], 1, 128, fp32=True)
                    p.op("vector", lambda e: e.reciprocal(rcp[:, :], rcp[:, :]))
                    for g2 in range(4):
                        for kt in range(NT):
                            p.mm(ps_pv[:, :], vh[:, kt, :], eST[:, kt, g2 * 512:(g2 + 1) * 512], start=(kt == 0), stop=(kt == NT - 1))
                        p.copy(attnTh[:, g2 * 512:(g2 + 1) * 512], ps_pv[:, :], q="vector")
                    for t in range(NT):
                        if h == 0:
                            p.memset(at[:, :], 0.0)
                        else:
                            p.dma(at[:, :], accd[t])
                        for g2 in range(2):
                            p.mm(ps_o[:, :], attnTh[:, t * 128:(t + 1) * 128], wo[:, h, g2 * 512:(g2 + 1) * 512], start=True, stop=True)
                            p.stt(at[:, g2 * 512:(g2 + 1) * 512], ps_o[:, :], rcp[:, t:t + 1], at[:, g2 * 512:(g2 + 1) * 512], "mult", "add")
                        p.dma(accd[t], at[:, :])
                for t in range(NT):
                    p.dma(at[:, :], accd[t])
                    p.dma(c.t0[:, :], xin[bi, t * 128:(t + 1) * 128, :])
                    p.tt(at[:, :], at[:, :], bo32[:, :], "add")
                    p.tt(at[:, :], at[:, :], c.t0[:, :], "add")
                    p.dma(xout[bi, t * 128:(t + 1) * 128, :], at[:, :])


def emit_l1attna(p, c, xin, xout, Wd, tag):
    B = c.B
    aTd = p.dram(tag + "aTd", (8, 128, SEQ), BF16)
    with p.scope():
        nw = p.sb("nw", (128, 8), F32)
        p.dma(nw[:, :], Wd["a1_nw_pg"][:, :])
        wq = p.sb("a1wq", (128, 8, 1024), BF16)
        wk = p.sb("a1wk", (128, 8, 512), BF16)
        wv = p.sb("a1wv", (128, 8, 512), BF16)
        wo = loadw_bf(p, c, "a1wo", Wd["a1_Wo"], 8, 1024)
        wf = p.sb("a1wf", (128, 1024), F32)
        for k in range(8):
            p.dma(wf[:, :], Wd["a1_Wq"][k * 128:(k + 1) * 128, :])
            p.ts(wf[:, :], wf[:, :], nw[:, k:k + 1], "mult")
            p.copy(wq[:, k, :], wf[:, :], q="vector")
            p.dma(wf[:, 0:512], Wd["a1_Wk"][k * 128:(k + 1) * 128, :])
            p.ts(wf[:, 0:512], wf[:, 0:512], nw[:, k:k + 1], "mult")
            p.copy(wk[:, k, :], wf[:, 0:512], q="vector")
            p.dma(wf[:, 0:512], Wd["a1_Wv"][k * 128:(k + 1) * 128, :])
            p.ts(wf[:, 0:512], wf[:, 0:512], nw[:, k:k + 1], "mult")
            p.copy(wv[:, k, :], wf[:, 0:512], q="vector")
        tmask = p.sb("tmask", (128, 128), F32)
        p.dma(tmask[:, :], Wd["c_tmask"][:, :])
        scol = scalar_col(p, c, "tempcol", Wd["a1_ltemp"], fn=lambda s: p.act(s[:, :], s[:, :], AF.Exp), slot=0)
        gcol = scalar_col(p, c, "gatecol", Wd["a1_gate"], slot=1)

        qT = p.sb("a1qT", (128, 8, SEQ), BF16)
        kTn = p.sb("a1kT", (128, 4, SEQ), BF16)
        vtm = p.sb("a1v", (128, NT, 512), BF16)
        qtm = p.sb("a1q", (128, 1024), F32)
        lb = p.sb("a1lb", (128, 1024), BF16)
        sc = p.sb("a1sc", (128, SEQ), F32)
        scb = p.sb("a1scb", (128, SEQ), BF16)
        PT = p.sb("a1PT", (128, NT, 128), BF16)
        ps1 = p.ps("a1ps1", (128, 512), F32)
        ps2 = p.ps("a1ps2", (128, 512), F32)

        for bi in range(B):
            with p.scope():
                hT = p.sb("a1hT", (128, 8, SEQ), BF16)
                for t in range(NT):
                    p.dma(c.t0[:, :], xin[bi, t * 128:(t + 1) * 128, :])
                    ln_tile(p, c, c.t0[:, :], c.t0[:, :])
                    p.act(c.t1[:, :], c.t0[:, :], AF.Square)
                    p.red(c.c0[:, :], c.t1[:, :], "add")
                    p.ts(c.c0[:, :], c.c0[:, :], 1.0 / D, "mult")
                    p.ts(c.c0[:, :], c.c0[:, :], 1e-6, "add")
                    p.act(c.c0[:, :], c.c0[:, :], AF.Ln)
                    p.ts(c.c0[:, :], c.c0[:, :], -0.5, "mult")
                    p.act(c.c0[:, :], c.c0[:, :], AF.Exp)
                    p.ts(c.t0[:, :], c.t0[:, :], c.c0[:, :], "mult")
                    p.copy(lb[:, :], c.t0[:, :], q="vector")
                    for g in range(8):
                        tr128(p, c, hT[:, g, t * 128:(t + 1) * 128], lb[:, g * 128:(g + 1) * 128], 128, 128)
                for t in range(NT):
                    for g2 in range(2):
                        for k in range(8):
                            p.mm(ps1[:, :], hT[:, k, t * 128:(t + 1) * 128], wq[:, k, g2 * 512:(g2 + 1) * 512], start=(k == 0), stop=(k == 7))
                        p.copy(qtm[:, g2 * 512:(g2 + 1) * 512], ps1[:, :])
                    p.act(c.t1[:, :], qtm[:, :], AF.Square)
                    for hh in range(4):
                        p.red(c.c0[:, :], c.t1[:, hh * 256:(hh + 1) * 256], "add")
                        p.act(c.c0[:, :], c.c0[:, :], AF.Sqrt)
                        p.ts(c.c0[:, :], c.c0[:, :], 1e-12, "max")
                        p.op("vector", lambda e: e.reciprocal(c.c0[:, :], c.c0[:, :]))
                        p.ts(qtm[:, hh * 256:(hh + 1) * 256], qtm[:, hh * 256:(hh + 1) * 256], c.c0[:, :], "mult")
                    p.copy(lb[:, :], qtm[:, :], q="vector")
                    for g in range(8):
                        tr128(p, c, qT[:, g, t * 128:(t + 1) * 128], lb[:, g * 128:(g + 1) * 128], 128, 128)
                for t in range(NT):
                    for k in range(8):
                        p.mm(ps1[:, :], hT[:, k, t * 128:(t + 1) * 128], wk[:, k, :], start=(k == 0), stop=(k == 7))
                    p.copy(qtm[:, 0:512], ps1[:, :])
                    p.act(c.t1[:, 0:512], qtm[:, 0:512], AF.Square)
                    for hh in range(2):
                        p.red(c.c0[:, :], c.t1[:, hh * 256:(hh + 1) * 256], "add")
                        p.act(c.c0[:, :], c.c0[:, :], AF.Sqrt)
                        p.ts(c.c0[:, :], c.c0[:, :], 1e-12, "max")
                        p.op("vector", lambda e: e.reciprocal(c.c0[:, :], c.c0[:, :]))
                        p.ts(qtm[:, hh * 256:(hh + 1) * 256], qtm[:, hh * 256:(hh + 1) * 256], c.c0[:, :], "mult")
                    p.copy(lb[:, 0:512], qtm[:, 0:512], q="vector")
                    for g in range(4):
                        tr128(p, c, kTn[:, g, t * 128:(t + 1) * 128], lb[:, g * 128:(g + 1) * 128], 128, 128)
                    for k in range(8):
                        p.mm(ps1[:, :], hT[:, k, t * 128:(t + 1) * 128], wv[:, k, :], start=(k == 0), stop=(k == 7))
                    p.copy(vtm[:, t, :], ps1[:, :], q="vector")
            for h in range(4):
                kv = h // 2
                for qt in range(NT):
                    kl = (qt + 1) * 128
                    for g2 in range((kl + 511) // 512):
                        f0 = g2 * 512
                        fl = min(512, kl - f0)
                        for j in range(2):
                            p.mm(ps1[:, 0:fl], qT[:, 2 * h + j, qt * 128:(qt + 1) * 128], kTn[:, 2 * kv + j, f0:f0 + fl], start=(j == 0), stop=(j == 1))
                        p.copy(sc[:, f0:f0 + fl], ps1[:, 0:fl])
                    p.ts(sc[:, 0:kl], sc[:, 0:kl], scol[:, :], "mult")
                    p.tt(sc[:, qt * 128:kl], sc[:, qt * 128:kl], tmask[:, :], "add")
                    p.red(c.c0[:, :], sc[:, 0:kl], "max", negate=True)
                    p.act(sc[:, 0:kl], sc[:, 0:kl], AF.Exp, bias=c.c0[:, :])
                    p.red(c.c1[:, :], sc[:, 0:kl], "add")
                    p.op("vector", lambda e: e.reciprocal(c.c1[:, :], c.c1[:, :]))
                    p.ts(sc[:, 0:kl], sc[:, 0:kl], c.c1[:, :], "mult")
                    p.copy(scb[:, 0:kl], sc[:, 0:kl], q="vector")
                    for kt in range(qt + 1):
                        tr128(p, c, PT[:, kt, :], scb[:, kt * 128:(kt + 1) * 128], 128, 128)
                    for j in range(2):
                        for kt in range(qt + 1):
                            p.mm(ps2[:, 0:128], vtm[:, kt, kv * 256 + j * 128: kv * 256 + (j + 1) * 128], PT[:, kt, :], start=(kt == 0), stop=(kt == qt))
                        p.copy(ev2 := sc[:, 0:128], ps2[:, 0:128])
                        p.copy(lb[:, 0:128], ev2, q="vector")
                        p.dma(aTd[2 * h + j, :, qt * 128:(qt + 1) * 128], lb[:, 0:128])
            with p.scope():
                aT = p.sb("a1aT", (128, 8, SEQ), BF16)
                lnt = p.sb("a1lnt", (128, 1024), F32)
                for g in range(8):
                    p.dma(aT[:, g, :], aTd[g])
                for t in range(NT):
                    p.dma(c.t0[:, :], xin[bi, t * 128:(t + 1) * 128, :])
                    ln_tile(p, c, lnt[:, :], c.t0[:, :])
                    p.tt(c.t0[:, :], c.t0[:, :], lnt[:, :], "add")
                    for g2 in range(2):
                        for k in range(8):
                            p.mm(ps1[:, :], aT[:, k, t * 128:(t + 1) * 128], wo[:, k, g2 * 512:(g2 + 1) * 512], start=(k == 0), stop=(k == 7))
                        p.stt(c.t0[:, g2 * 512:(g2 + 1) * 512], ps1[:, :], gcol[:, :], c.t0[:, g2 * 512:(g2 + 1) * 512], "mult", "add")
                    p.dma(xout[bi, t * 128:(t + 1) * 128, :], c.t0[:, :])

def emit_l0ffn(p, c, xin, xout, Wd, tag):
    B = c.B
    with p.scope():
        w1aT = loadw_bf(p, c, "w1aT", Wd["f0_w1aT"], 8, 512)
        w2T = loadw_bf(p, c, "w2T", Wd["f0_w2T"], 4, 64)
        wvT = loadw_bf(p, c, "wvT", Wd["f0_wvT"], 8, 1024)
        woT = loadw_bf(p, c, "woT", Wd["f0_woT"], 8, 1024)
        b2b, _ = loadrow_bf(p, "b2r", Wd["f0_b2"], 64)
        bvb, _ = loadrow_bf(p, "bvr0", Wd["f0_bv"], 1024)
        bob, _ = loadrow_bf(p, "bor0", Wd["f0_bo"], 1024)
        def taufn(s):
            p.act(s[:, :], s[:, :], AF.Abs)
            p.ts(s[:, :], s[:, :], 0.1, "add")
            p.op("vector", lambda e: e.reciprocal(s[:, :], s[:, :]))
        taucol = scalar_col(p, c, "taucol", Wd["f0_tau"], fn=taufn, slot=2)
        S = p.sb("slotS", (64, 1024), F32)
        scl = p.sb("sscale", (64, 1), F32)
        p.dma(S[:, :], Wd["f0_sinit"][:, :])
        p.dma(scl[:, :], Wd["f0_sscale"][:, :])
        p.ts(S[:, :], S[:, :], scl[:, :], "mult")
        Sb = p.sb("slotSb", (64, 1024), BF16)
        p.copy(Sb[:, :], S[:, :], q="vector")
        gh = p.sb("gh", (64, 3072), F32)
        with p.scope():
            whhT = loadw_bf(p, c, "whhT", Wd["f0_whhT"], 8, 3072)
            bhhb, _ = loadrow_bf(p, "bhhr", Wd["f0_bhh"], 3072)
            hsT = p.sb("hsT", (128, 8, 64), BF16)
            for k in range(8):
                tr128(p, c, hsT[:, k, :], Sb[:, k * 128:(k + 1) * 128], 64, 128)
            psg = p.ps("psg", (64, 512), F32)
            for grp in range(6):
                for k in range(8):
                    p.mm(psg[:, :], hsT[:, k, :], whhT[:, k, grp * 512:(grp + 1) * 512], start=(k == 0), stop=False)
                p.mm(psg[:, :], c.ones1b[:, 0:64], bhhb[:, grp * 512:(grp + 1) * 512], start=False, stop=True)
                p.copy(gh[:, grp * 512:(grp + 1) * 512], psg[:, :])
        biasTc = p.sb("biasTc", (128, 4), F32)
        with p.scope():
            w1bT = loadw_bf(p, c, "w1bT", Wd["f0_w1bT"], 8, 512)
            psm = p.ps("psm", (64, 512), F32)
            sm = p.sb("sm", (1, 1024), F32)
            for grp in range(2):
                p.mm(psm[0:1, :], c.onescolb[0:64, :], Sb[:, grp * 512:(grp + 1) * 512], start=True, stop=True)
                p.act(sm[:, grp * 512:(grp + 1) * 512], psm[0:1, :], AF.Identity, scale=1.0 / 64.0)
            smT = p.sb("smT", (128, 8, 1), F32)
            smTb = p.sb("smTb", (128, 8, 1), BF16)
            for k in range(8):
                tr128(p, c, smT[:, k, :], sm[0:1, k * 128:(k + 1) * 128], 1, 128, fp32=True)
            p.copy(smTb[:, :, :], smT[:, :, :], q="vector")
            br = p.sb("brow", (1, 512), F32)
            b1f = p.sb("b1f", (1, 512), F32)
            p.dma(b1f[:, :], Wd["f0_b1"][:, :])
            for k in range(8):
                p.mm(psm[0:1, :], smTb[:, k, :], w1bT[:, k, :], start=(k == 0), stop=(k == 7))
            p.copy(br[:, :], psm[0:1, :])
            p.tt(br[:, :], br[:, :], b1f[:, :], "add")
            for j in range(4):
                tr128(p, c, biasTc[:, j:j + 1], br[0:1, j * 128:(j + 1) * 128], 1, 128, fp32=True)

        alphaT = p.sb("alphaT", (64, NT, 128), BF16)
        gi = p.sb("gi", (64, 1024), F32)
        svb = p.sb("svb", (64, 1024), BF16)
        lg = p.sb("lg0", (128, 64), F32)
        wkk = p.sb("wk0", (128, 64), F32)
        kp = p.sb("kp0", (128, 64), F32)
        mx = p.sb("mx0", (128, 8), F32)
        alb = p.sb("alb", (128, 64), BF16)
        lnb = p.sb("lnb0", (128, 1024), BF16)
        xTt = p.sb("xTt0", (128, 8, 128), BF16)
        hTb = p.sb("hTb0", (128, 4, 128), BF16)

        for bi in range(B):
            with p.scope():
                si0 = p.ps("si0", (64, 512), F32)
                si1 = p.ps("si1", (64, 512), F32)
                ws = p.ps("ws0", (64, 1), F32)
                ps_h = p.ps("ps_h0", (128, 128), F32)
                ps_lg = p.ps("ps_lg", (128, 64), F32)
                for t in range(NT):
                    p.dma(c.t0[:, :], xin[bi, t * 128:(t + 1) * 128, :])
                    ln_tile(p, c, c.t0[:, :], c.t0[:, :])
                    p.copy(lnb[:, :], c.t0[:, :], q="vector")
                    for g in range(8):
                        tr128(p, c, xTt[:, g, :], lnb[:, g * 128:(g + 1) * 128], 128, 128)
                    for ht in range(4):
                        for k in range(8):
                            p.mm(ps_h[:, :], w1aT[:, k, ht * 128:(ht + 1) * 128], xTt[:, k, :], start=(k == 0), stop=(k == 7))
                        p.act(hTb[:, ht, :], ps_h[:, :], AF.Gelu, bias=biasTc[:, ht:ht + 1])
                    for k in range(4):
                        p.mm(ps_lg[:, :], hTb[:, k, :], w2T[:, k, :], start=(k == 0), stop=False)
                    p.mm(ps_lg[:, :], c.ones1b[:, :], b2b[:, :], start=False, stop=True)
                    p.copy(lg[:, :], ps_lg[:, :])
                    p.ts(lg[:, :], lg[:, :], taucol[:, :], "mult")
                    p.copy(wkk[:, :], lg[:, :], q="vector")
                    for i in range(8):
                        p.red(mx[:, i:i + 1], wkk[:, :], "max")
                        if i < 7:
                            p.ts(kp[:, :], wkk[:, :], mx[:, i:i + 1], "is_ge")
                            p.stt(wkk[:, :], kp[:, :], -1e30, wkk[:, :], "mult", "add")
                    p.ts(c.c0[:, :], mx[:, 0:1], -1.0, "mult")
                    p.ts(kp[:, :], lg[:, :], mx[:, 7:8], "is_ge")
                    p.act(lg[:, :], lg[:, :], AF.Exp, bias=c.c0[:, :])
                    p.tt(lg[:, :], lg[:, :], kp[:, :], "mult")
                    p.red(c.c1[:, :], lg[:, :], "add")
                    p.op("vector", lambda e: e.reciprocal(c.c1[:, :], c.c1[:, :]))
                    p.ts(lg[:, :], lg[:, :], c.c1[:, :], "mult")
                    p.copy(alb[:, :], lg[:, :], q="vector")
                    tr128(p, c, alphaT[:, t, :], alb[:, :], 128, 64)
                    p.mm(si0[:, :], alb[:, :], lnb[:, 0:512], start=(t == 0), stop=(t == NT - 1))
                    p.mm(si1[:, :], alb[:, :], lnb[:, 512:1024], start=(t == 0), stop=(t == NT - 1))
                    p.mm(ws[:, :], alb[:, :], c.onescolb[:, :], start=(t == 0), stop=(t == NT - 1))
                p.copy(gi[:, 0:512], si0[:, :])
                p.copy(gi[:, 512:1024], si1[:, :])
                wse = p.sb("wse", (64, 1), F32)
                p.copy(wse[:, :], ws[:, :])
                p.ts(wse[:, :], wse[:, :], 1e-8, "add")
                p.op("vector", lambda e: e.reciprocal(wse[:, :], wse[:, :]))
                p.ts(gi[:, :], gi[:, :], wse[:, :], "mult")
            with p.scope():
                gib = p.sb("gib", (64, 1024), BF16)
                p.copy(gib[:, :], gi[:, :], q="vector")
                giT = p.sb("giT", (128, 8, 64), BF16)
                for k in range(8):
                    tr128(p, c, giT[:, k, :], gib[:, k * 128:(k + 1) * 128], 64, 128)
                wihT = loadw_bf(p, c, "wihT", Wd["f0_wihT"], 8, 3072)
                bihb, _ = loadrow_bf(p, "bihr", Wd["f0_bih"], 3072)
                gx = p.sb("gx", (64, 3072), F32)
                psg2 = p.ps("psg2", (64, 512), F32)
                for grp in range(6):
                    for k in range(8):
                        p.mm(psg2[:, :], giT[:, k, :], wihT[:, k, grp * 512:(grp + 1) * 512], start=(k == 0), stop=False)
                    p.mm(psg2[:, :], c.ones1b[:, 0:64], bihb[:, grp * 512:(grp + 1) * 512], start=False, stop=True)
                    p.copy(gx[:, grp * 512:(grp + 1) * 512], psg2[:, :])
                rr = p.sb("rr", (64, 1024), F32)
                zz = p.sb("zz", (64, 1024), F32)
                nn = p.sb("nn", (64, 1024), F32)
                tmp = p.sb("tmpg", (64, 1024), F32)
                p.tt(rr[:, :], gx[:, 0:1024], gh[:, 0:1024], "add")
                p.act(rr[:, :], rr[:, :], AF.Sigmoid)
                p.tt(zz[:, :], gx[:, 1024:2048], gh[:, 1024:2048], "add")
                p.act(zz[:, :], zz[:, :], AF.Sigmoid)
                p.tt(tmp[:, :], rr[:, :], gh[:, 2048:3072], "mult")
                p.tt(nn[:, :], gx[:, 2048:3072], tmp[:, :], "add")
                p.act(nn[:, :], nn[:, :], AF.Tanh)
                p.tt(tmp[:, :], S[:, :], nn[:, :], "subtract")
                p.tt(tmp[:, :], zz[:, :], tmp[:, :], "mult")
                p.tt(nn[:, :], nn[:, :], tmp[:, :], "add")
                hnb = p.sb("hnb", (64, 1024), BF16)
                p.copy(hnb[:, :], nn[:, :], q="vector")
                hnT = p.sb("hnT", (128, 8, 64), BF16)
                for k in range(8):
                    tr128(p, c, hnT[:, k, :], hnb[:, k * 128:(k + 1) * 128], 64, 128)
                for grp in range(2):
                    for k in range(8):
                        p.mm(psg2[:, :], hnT[:, k, :], wvT[:, k, grp * 512:(grp + 1) * 512], start=(k == 0), stop=False)
                    p.mm(psg2[:, :], c.ones1b[:, 0:64], bvb[:, grp * 512:(grp + 1) * 512], start=False, stop=True)
                    p.copy(svb[:, grp * 512:(grp + 1) * 512], psg2[:, :], q="vector")
            with p.scope():
                ps_o = p.ps("ps_o0", (128, 512), F32)
                ps_w = p.ps("ps_w0", (128, 512), F32)
                obf = p.sb("obf", (128, 1024), BF16)
                oTt = p.sb("oTt", (128, 8, 128), BF16)
                for t in range(NT):
                    for g2 in range(2):
                        p.mm(ps_o[:, :], alphaT[:, t, :], svb[:, g2 * 512:(g2 + 1) * 512], start=True, stop=True)
                        p.copy(obf[:, g2 * 512:(g2 + 1) * 512], ps_o[:, :], q="vector")
                    for g in range(8):
                        tr128(p, c, oTt[:, g, :], obf[:, g * 128:(g + 1) * 128], 128, 128)
                    p.dma(c.t0[:, :], xin[bi, t * 128:(t + 1) * 128, :])
                    for g2 in range(2):
                        for k in range(8):
                            p.mm(ps_w[:, :], oTt[:, k, :], woT[:, k, g2 * 512:(g2 + 1) * 512], start=(k == 0), stop=False)
                        p.mm(ps_w[:, :], c.ones1b[:, :], bob[:, g2 * 512:(g2 + 1) * 512], start=False, stop=True)
                        p.tt(c.t0[:, g2 * 512:(g2 + 1) * 512], ps_w[:, :], c.t0[:, g2 * 512:(g2 + 1) * 512], "add")
                    p.dma(xout[bi, t * 128:(t + 1) * 128, :], c.t0[:, :])


MAGIC = 12582912.0
INV2PI = 1.0 / (2.0 * np.pi)
TWOPI = 2.0 * np.pi


def emit_l0attnb(p, c, xin, xout, Wd, tag):
    B = c.B
    Q = 64
    NCH = SEQ // Q
    ufm = p.dram(tag + "ufm", (B, 8, 128, SEQ), F32)
    ygel = p.dram(tag + "ygel", (B, 8, 128, SEQ), BF16)
    with p.scope():
        ev = p.sb("ev0", (128, 128), F32)
        for bi in range(B):
            for t in range(NT):
                p.dma(c.t0[:, :], xin[bi, t * 128:(t + 1) * 128, :])
                ln_tile(p, c, c.t0[:, :], c.t0[:, :])
                for g in range(8):
                    tr128(p, c, ev[:, :], c.t0[:, g * 128:(g + 1) * 128], 128, 128, fp32=True)
                    p.dma(ufm[bi, g, :, t * 128:(t + 1) * 128], ev[:, :])
    with p.scope():
        G = lambda name: p.sb(name, (128, 8, 32), F32)
        dt = p.sb("dt", (128, 8), F32)
        p.dma(dt[:, :], Wd["b0_logdt_pg"][:, :])
        p.act(dt[:, :], dt[:, :], AF.Exp)
        we = p.sb("we", (128, 32), F32)
        wim = p.sb("wim", (128, 32), F32)
        p.dma(we[:, :], Wd["b0_lwr"].broadcast_to([128, 32]))
        p.act(we[:, :], we[:, :], AF.Exp)
        p.dma(wim[:, :], Wd["b0_wim"].broadcast_to([128, 32]))
        dtv = dt.rearrange("pp (g o) -> pp g o", o=1).broadcast_to([128, 8, 32])
        wev = we.rearrange("pp (o n) -> pp o n", o=1).broadcast_to([128, 8, 32])
        wimv = wim.rearrange("pp (o n) -> pp o n", o=1).broadcast_to([128, 8, 32])
        dAr, dAi, E, sinb, cosb = G("dAr"), G("dAi"), G("E"), G("sinb"), G("cosb")
        t1g, t2g = G("t1g"), G("t2g")
        p.tt(dAr[:, :, :], dtv, wev, "mult")
        p.ts(dAr[:, :, :], dAr[:, :, :], -1.0, "mult")
        p.tt(dAi[:, :, :], dtv, wimv, "mult")
        p.act(E[:, :, :], dAr[:, :, :], AF.Exp)

        def redsin(dst, src_done_in_t1g):
            p.ts(t2g[:, :, :], t1g[:, :, :], INV2PI, "mult")
            p.ts(t2g[:, :, :], t2g[:, :, :], MAGIC, "add")
            p.ts(t2g[:, :, :], t2g[:, :, :], -MAGIC, "add")
            p.ts(t2g[:, :, :], t2g[:, :, :], -TWOPI, "mult")
            p.stt(t2g[:, :, :], t1g[:, :, :], 1.0, t2g[:, :, :], "mult", "add")
            p.act(dst, t2g[:, :, :], AF.Sin)

        p.copy(t1g[:, :, :], dAi[:, :, :], q="vector")
        redsin(sinb[:, :, :], None)
        p.ts(t1g[:, :, :], dAi[:, :, :], np.pi / 2.0, "add")
        redsin(cosb[:, :, :], None)
        zr, zi = G("zr"), G("zi")
        p.tt(zr[:, :, :], E[:, :, :], cosb[:, :, :], "mult")
        p.tt(zi[:, :, :], E[:, :, :], sinb[:, :, :], "mult")
        Cr, Ci = G("Cr"), G("Ci")
        p.dma(Cr[:, :, :], Wd["b0_Cr"].rearrange("(g pp) n -> pp g n", pp=128))
        p.dma(Ci[:, :, :], Wd["b0_Ci"].rearrange("(g pp) n -> pp g n", pp=128))
        den = p.sb("den", (128, 32), F32)
        p.act(den[:, :], we[:, :], AF.Square)
        p.act(t2g[:, 0, :], wim[:, :], AF.Square)
        p.tt(den[:, :], den[:, :], t2g[:, 0, :], "add")
        p.op("vector", lambda e: e.reciprocal(den[:, :], den[:, :]))
        denv = den.rearrange("pp (o n) -> pp o n", o=1).broadcast_to([128, 8, 32])
        em1r = t1g
        p.ts(em1r[:, :, :], zr[:, :, :], 1.0, "subtract")
        qr, qi = G("qr"), G("qi")
        p.tt(qr[:, :, :], em1r[:, :, :], wev, "mult")
        p.tt(t2g[:, :, :], zi[:, :, :], wimv, "mult")
        p.tt(qr[:, :, :], t2g[:, :, :], qr[:, :, :], "subtract")
        p.tt(qr[:, :, :], qr[:, :, :], denv, "mult")
        p.tt(qi[:, :, :], em1r[:, :, :], wimv, "mult")
        p.tt(t2g[:, :, :], zi[:, :, :], wev, "mult")
        p.tt(qi[:, :, :], qi[:, :, :], t2g[:, :, :], "add")
        p.tt(qi[:, :, :], qi[:, :, :], denv, "mult")
        p.ts(qi[:, :, :], qi[:, :, :], -1.0, "mult")
        Cwr, Cwi, nCwi = G("Cwr"), G("Cwi"), G("nCwi")
        p.tt(Cwr[:, :, :], Cr[:, :, :], qr[:, :, :], "mult")
        p.tt(t2g[:, :, :], Ci[:, :, :], qi[:, :, :], "mult")
        p.tt(Cwr[:, :, :], Cwr[:, :, :], t2g[:, :, :], "subtract")
        p.tt(Cwi[:, :, :], Cr[:, :, :], qi[:, :, :], "mult")
        p.tt(t2g[:, :, :], Ci[:, :, :], qr[:, :, :], "mult")
        p.tt(Cwi[:, :, :], Cwi[:, :, :], t2g[:, :, :], "add")
        p.ts(Cwr[:, :, :], Cwr[:, :, :], 2.0, "mult")
        p.ts(Cwi[:, :, :], Cwi[:, :, :], 2.0, "mult")
        p.ts(nCwi[:, :, :], Cwi[:, :, :], -1.0, "mult")
        Dcol = p.sb("Dcol", (128, 8), F32)
        p.dma(Dcol[:, :], Wd["b0_D_pg"][:, :])

        Zpr = p.sb("Zpr", (128, 32, 65), F32)
        Zpi = p.sb("Zpi", (128, 32, 65), F32)
        Zrr = p.sb("Zrr", (128, 32, 64), F32)
        Zri = p.sb("Zri", (128, 32, 64), F32)
        ztmp = p.sb("ztmp", (128, 32, 32), F32)
        ztmp2 = p.sb("ztmp2", (128, 32, 32), F32)
        kg = p.sb("kg", (128, 64), F32)
        NB = min(2, B)
        u_all = p.sb("u_all", (128, NB, SEQ), F32)
        y_all = p.sb("y_all", (128, NB, SEQ), F32)
        big = p.sb("big", (128, NB, SEQ), F32)
        Pr = p.sb("Pr", (128, 32, NB, NCH), F32)
        Pi = p.sb("Pi", (128, 32, NB, NCH), F32)
        Sar = p.sb("Sar", (128, 32, NB, NCH), F32)
        Sai = p.sb("Sai", (128, 32, NB, NCH), F32)
        Sr = p.sb("Sr", (128, 32, NB), F32)
        Si = p.sb("Si", (128, 32, NB), F32)
        sc1 = p.sb("sc1", (128, 32, NB), F32)
        sc2 = p.sb("sc2", (128, 32, NB), F32)
        ygb = p.sb("ygb", (128, SEQ), BF16)
        for bh, g in [(bh, g) for bh in range(0, B, NB) for g in range(8)]:
                p.memset(Zpr[:, :, 0:1], 1.0)
                p.memset(Zpi[:, :, 0:1], 0.0)
                p.copy(Zpr[:, :, 1:2], zr[:, g, :].rearrange("pp (n o) -> pp n o", o=1), q="vector")
                p.copy(Zpi[:, :, 1:2], zi[:, g, :].rearrange("pp (n o) -> pp n o", o=1), q="vector")
                for m in [1, 2, 4, 8, 16, 32]:
                    br_ = Zpr[:, :, m:m + 1].broadcast_to([128, 32, m])
                    bi_ = Zpi[:, :, m:m + 1].broadcast_to([128, 32, m])
                    p.tt(ztmp[:, :, 0:m], Zpr[:, :, 1:m + 1], br_, "mult")
                    p.tt(ztmp2[:, :, 0:m], Zpi[:, :, 1:m + 1], bi_, "mult")
                    p.tt(Zpr[:, :, m + 1:2 * m + 1], ztmp[:, :, 0:m], ztmp2[:, :, 0:m], "subtract")
                    p.tt(ztmp[:, :, 0:m], Zpr[:, :, 1:m + 1], bi_, "mult")
                    p.tt(ztmp2[:, :, 0:m], Zpi[:, :, 1:m + 1], br_, "mult")
                    p.tt(Zpi[:, :, m + 1:2 * m + 1], ztmp[:, :, 0:m], ztmp2[:, :, 0:m], "add")
                for i in range(64):
                    p.copy(Zrr[:, :, i:i + 1], Zpr[:, :, 63 - i:64 - i], q="vector")
                    p.copy(Zri[:, :, i:i + 1], Zpi[:, :, 63 - i:64 - i], q="vector")
                p.memset(kg[:, :], 0.0)
                for n in range(32):
                    p.stt(kg[:, :], Zpr[:, n, 0:64], Cwr[:, g, n:n + 1], kg[:, :], "mult", "add")
                    p.stt(kg[:, :], Zpi[:, n, 0:64], nCwi[:, g, n:n + 1], kg[:, :], "mult", "add")
                if getattr(p, "dbg_b0", False) and bh == 0:
                    ko_ = p.out(tag + "_dbg_kg%d" % g, (128, 64), F32)
                    p.dma(ko_, kg[:, :])
                for bi in range(NB):
                    p.dma(u_all[:, bi, :], ufm[bh + bi, g])
                uv = u_all.rearrange("pp b (cc i) -> pp b cc i", i=Q)
                yv = y_all.rearrange("pp b (cc i) -> pp b cc i", i=Q)
                p.ts(yv[:, :, :, :], uv[:, :, :, :], kg[:, 0:1], "mult")
                for d in range(1, 64):
                    p.stt(yv[:, :, :, d:], uv[:, :, :, 0:Q - d], kg[:, d:d + 1], yv[:, :, :, d:], "mult", "add")
                bigv = big.rearrange("pp b (cc i) -> pp b cc i", i=Q)
                for n in range(32):
                    zv = Zrr[:, n, :].rearrange("pp (o i) -> pp o i", o=1).rearrange("pp o (oo i) -> pp o oo i", oo=1).broadcast_to([128, NB, NCH, Q])
                    p.tt(bigv[:, :, :, :], uv[:, :, :, :], zv, "mult")
                    p.red(Pr[:, n, :, :], bigv[:, :, :, :], "add")
                    zv2 = Zri[:, n, :].rearrange("pp (o i) -> pp o i", o=1).rearrange("pp o (oo i) -> pp o oo i", oo=1).broadcast_to([128, NB, NCH, Q])
                    p.tt(bigv[:, :, :, :], uv[:, :, :, :], zv2, "mult")
                    p.red(Pi[:, n, :, :], bigv[:, :, :, :], "add")
                p.memset(Sr[:, :, :], 0.0)
                p.memset(Si[:, :, :], 0.0)
                zQr = Zpr[:, :, 64:65].broadcast_to([128, 32, NB])
                zQi = Zpi[:, :, 64:65].broadcast_to([128, 32, NB])
                for ch in range(NCH):
                    p.copy(Sar[:, :, :, ch], Sr[:, :, :], q="vector")
                    p.copy(Sai[:, :, :, ch], Si[:, :, :], q="vector")
                    p.tt(sc1[:, :, :], Sr[:, :, :], zQr, "mult")
                    p.tt(sc2[:, :, :], Si[:, :, :], zQi, "mult")
                    p.tt(sc1[:, :, :], sc1[:, :, :], sc2[:, :, :], "subtract")
                    p.tt(sc2[:, :, :], Si[:, :, :], zQr, "mult")
                    p.tt(Si[:, :, :], Sr[:, :, :], zQi, "mult")
                    p.tt(Si[:, :, :], Si[:, :, :], sc2[:, :, :], "add")
                    p.tt(Si[:, :, :], Si[:, :, :], Pi[:, :, :, ch], "add")
                    p.tt(Sr[:, :, :], sc1[:, :, :], Pr[:, :, :, ch], "add")
                Cwrv = Cwr[:, g, :].rearrange("pp (n o) -> pp n o", o=1).rearrange("pp n (o oo) -> pp n o oo", oo=1).broadcast_to([128, 32, NB, NCH])
                Cwiv = Cwi[:, g, :].rearrange("pp (n o) -> pp n o", o=1).rearrange("pp n (o oo) -> pp n o oo", oo=1).broadcast_to([128, 32, NB, NCH])
                p.tt(Pr[:, :, :, :], Sar[:, :, :, :], Cwrv, "mult")
                p.tt(Pi[:, :, :, :], Sai[:, :, :, :], Cwiv, "mult")
                p.tt(Pr[:, :, :, :], Pr[:, :, :, :], Pi[:, :, :, :], "subtract")
                p.tt(Pi[:, :, :, :], Sar[:, :, :, :], Cwiv, "mult")
                p.tt(Sar[:, :, :, :], Sai[:, :, :, :], Cwrv, "mult")
                p.tt(Pi[:, :, :, :], Pi[:, :, :, :], Sar[:, :, :, :], "add")
                for n in range(32):
                    Arv = Pr[:, n, :, :].rearrange("pp b (cc o) -> pp b cc o", o=1).broadcast_to([128, NB, NCH, Q])
                    Aiv = Pi[:, n, :, :].rearrange("pp b (cc o) -> pp b cc o", o=1).broadcast_to([128, NB, NCH, Q])
                    z1r = Zpr[:, n, 1:65].rearrange("pp (o i) -> pp o i", o=1).rearrange("pp o (oo i) -> pp o oo i", oo=1).broadcast_to([128, NB, NCH, Q])
                    z1i = Zpi[:, n, 1:65].rearrange("pp (o i) -> pp o i", o=1).rearrange("pp o (oo i) -> pp o oo i", oo=1).broadcast_to([128, NB, NCH, Q])
                    p.tt(bigv[:, :, :, :], Arv, z1r, "mult")
                    p.tt(yv[:, :, :, :], yv[:, :, :, :], bigv[:, :, :, :], "add")
                    p.tt(bigv[:, :, :, :], Aiv, z1i, "mult")
                    p.tt(yv[:, :, :, :], yv[:, :, :, :], bigv[:, :, :, :], "subtract")
                for bi in range(NB):
                    p.stt(y_all[:, bi, :], u_all[:, bi, :], Dcol[:, g:g + 1], y_all[:, bi, :], "mult", "add")
                    p.act(y_all[:, bi, :], y_all[:, bi, :], AF.Gelu)
                    p.copy(ygb[:, :], y_all[:, bi, :], q="vector")
                    p.dma(ygel[bh + bi, g], ygb[:, :])
    if getattr(p, "dbg_b0", False):
        yo_ = p.out(tag + "_dbg_ygel", (B, 8, 128, SEQ), BF16)
        with p.scope():
            tmpy = p.sb("tmpy", (128, SEQ), BF16)
            for bi in range(B):
                for g in range(8):
                    p.dma(tmpy[:, :], ygel[bi, g])
                    p.dma(yo_[bi, g], tmpy[:, :])
    with p.scope():
        woT = loadw_bf(p, c, "b0woT", Wd["b0_WoT"], 8, 1024)
        bob, _ = loadrow_bf(p, "b0bor", Wd["b0_bo"], 1024)
        yg = p.sb("yg", (128, 8, SEQ), BF16)
        ps_w = p.ps("ps_wb0", (128, 512), F32)
        for bi in range(B):
            for g in range(8):
                p.dma(yg[:, g, :], ygel[bi, g])
            for t in range(NT):
                p.dma(c.t0[:, :], xin[bi, t * 128:(t + 1) * 128, :])
                for g2 in range(2):
                    for k in range(8):
                        p.mm(ps_w[:, :], yg[:, k, t * 128:(t + 1) * 128], woT[:, k, g2 * 512:(g2 + 1) * 512], start=(k == 0), stop=False)
                    p.mm(ps_w[:, :], c.ones1b[:, :], bob[:, g2 * 512:(g2 + 1) * 512], start=False, stop=True)
                    p.tt(c.t0[:, g2 * 512:(g2 + 1) * 512], ps_w[:, :], c.t0[:, g2 * 512:(g2 + 1) * 512], "add")
                p.dma(xout[bi, t * 128:(t + 1) * 128, :], c.t0[:, :])

def _row_spans(r0, n):
    out = []
    off = 0
    while n > 0:
        t = r0 // 128
        pp = r0 % 128
        cnt = min(n, 128 - pp)
        out.append((t, pp, cnt, off))
        r0 += cnt
        off += cnt
        n -= cnt
    return out


def emit_l0attna(p, c, xin, xout, Wd, tag):
    B = c.B
    NC = 32
    CK = 64
    Yhat = p.dram(tag + "Yhat", (B, NC, 66, 1024), F32)
    Htm = p.dram(tag + "Htm", (B, 2, 1024, 1024), F32)
    gd = p.dram(tag + "gd", (B, NC, 1024), F32)
    xld = p.dram(tag + "xld", (B, 2, 128, 1024), F32)
    alfd = p.dram(tag + "alfd", (B, 2, 128, 1024), F32)

    with p.scope():
        dftc = p.sb("dftc", (64, 66), F32)
        p.dma(dftc[:, :], Wd["c_dft"][:, :])
        pwrT = loadw_bf(p, c, "pwrT", Wd["a0_pwrT"], 8, 1024)
        pwiT = loadw_bf(p, c, "pwiT", Wd["a0_pwiT"], 8, 1024)
        gwT = loadw_bf(p, c, "gwT", Wd["a0_gwT"], 8, 1024)
        dwr = p.sb("dwr", (128, 8, 7), F32)
        dwi = p.sb("dwi", (128, 8, 7), F32)
        p.dma(dwr[:, :, :], Wd["a0_dwr"].rearrange("(g pp) kk -> pp g kk", pp=128))
        p.dma(dwi[:, :, :], Wd["a0_dwi"].rearrange("(g pp) kk -> pp g kk", pp=128))
        ps_d = p.ps("ps_d", (128, 512), F32)
        XF = p.sb("XF", (66, 1024), F32)
        xlr = p.sb("xlr", (128, 1024), F32)
        xli = p.sb("xli", (128, 1024), F32)
        xh_r = p.sb("xh_r", (128, 8, NC, 29), F32)
        xh_i = p.sb("xh_i", (128, 8, NC, 29), F32)
        cacc = p.sb("cacc", (128, NC, 29), F32)
        hcb = p.sb("hcb", (128, 8, NC * 29), BF16)
        gm = p.sb("gm", (32, 1024), F32)
        gmb = p.sb("gmb", (32, 1024), BF16)
        gmT = p.sb("gmT", (128, 8, 32), BF16)
        for bi in range(B):
            for t in range(NT):
                p.dma(c.t0[:, :], xin[bi, t * 128:(t + 1) * 128, :])
                ln_tile(p, c, c.t0[:, :], c.t0[:, :])
                for half in range(2):
                    ch = 2 * t + half
                    p.copy(c.t1[0:64, :], c.t0[half * 64:(half + 1) * 64, :], q="vector")
                    for g2 in range(2):
                        p.mm(ps_d[0:66, :], dftc[:, :], c.t1[0:64, g2 * 512:(g2 + 1) * 512], start=True, stop=True)
                        p.copy(XF[:, g2 * 512:(g2 + 1) * 512], ps_d[0:66, :])
                    p.dma(Yhat[bi, ch], XF[:, :])
                    p.dma(xlr[ch * 4:(ch + 1) * 4, :], XF[0:4, :])
                    p.dma(xli[ch * 4:(ch + 1) * 4, :], XF[33:37, :])
                    for g in range(8):
                        p.trans(c.pst_f[0:128, 0:66], XF[:, g * 128:(g + 1) * 128], c.ident[0:66, 0:66])
                        p.copy(xh_r[:, g, ch, :], c.pst_f[:, 4:33], q="vector")
                        p.copy(xh_i[:, g, ch, :], c.pst_f[:, 37:66], q="vector")
            p.dma(xld[bi, 0], xlr[:, :])
            p.dma(xld[bi, 1], xli[:, :])
            for ri, (xh, dw) in enumerate([(xh_r, dwr), (xh_i, dwi)]):
                for g in range(8):
                    p.ts(cacc[:, :, :], xh[:, g, :, :], dw[:, g, 6:7], "mult")
                    for j in range(6):
                        s = 6 - j
                        p.stt(cacc[:, :, s:29], xh[:, g, :, 0:29 - s], dw[:, g, j:j + 1], cacc[:, :, s:29], "mult", "add")
                    p.act(cacc[:, :, :], cacc[:, :, :], AF.Gelu)
                    p.copy(hcb[:, g, :], cacc.rearrange("pp a b -> pp (a b)"), q="vector")
                pwT = pwrT if ri == 0 else pwiT
                for mt in range(8):
                    rows = min(128, NC * 29 - mt * 128)
                    if rows <= 0:
                        break
                    for g2 in range(2):
                        for k in range(8):
                            p.mm(ps_d[0:rows, :], hcb[:, k, mt * 128:mt * 128 + rows], pwT[:, k, g2 * 512:(g2 + 1) * 512], start=(k == 0), stop=(k == 7))
                        p.copy(c.t1[0:rows, g2 * 512:(g2 + 1) * 512], ps_d[0:rows, :])
                    p.dma(Htm[bi, ri, mt * 128:mt * 128 + rows, :], c.t1[0:rows, :])
            p.dma(gm[0:32, :].rearrange("cc (o dd) -> cc o dd", o=1), xld[bi, 0].rearrange("(cc ff) dd -> cc ff dd", ff=4)[:, 0:1, :])
            p.act(gmb[:, :], gm[:, :], AF.Identity, scale=1.0 / 64.0)
            for g in range(8):
                tr128(p, c, gmT[:, g, :], gmb[:, g * 128:(g + 1) * 128], 32, 128)
            for g2 in range(2):
                for k in range(8):
                    p.mm(ps_d[0:32, :], gmT[:, k, :], gwT[:, k, g2 * 512:(g2 + 1) * 512], start=(k == 0), stop=(k == 7))
                p.act(gm[:, g2 * 512:(g2 + 1) * 512], ps_d[0:32, :], AF.Sigmoid)
            p.dma(gd[bi], gm[:, :])

    with p.scope():
        qwT = loadw_bf(p, c, "qwT", Wd["a0_qwT"], 8, 1024)
        kwT = loadw_bf(p, c, "kwT", Wd["a0_kwT"], 8, 256)
        vwT = loadw_bf(p, c, "vwT", Wd["a0_vwT"], 8, 256)
        owT = loadw_bf(p, c, "owT", Wd["a0_owT"], 8, 1024)
        bmask = p.sb("bmask", (128, 128), F32)
        p.dma(bmask[:, :], Wd["c_bmask"][:, :])
        acol = scalar_col(p, c, "acol", Wd["a0_alpha"], fn=lambda s: p.act(s[:, :], s[:, :], AF.Sigmoid), slot=3)
        xf = p.sb("xf0", (128, 1024), F32)
        xfb = p.sb("xfb0", (128, 1024), BF16)
        xfT = p.sb("xfT0", (128, 8, 128), BF16)
        q_sb = p.sb("q_sb", (128, 1024), BF16)
        kT_hs = p.sb("kT_hs", (64, 4, 128), BF16)
        v_sb = p.sb("v_sb", (128, 256), BF16)
        qTh = p.sb("qTh", (64, 128), BF16)
        aT = p.sb("aT0", (128, 8, 128), BF16)
        sc128 = p.sb("sc128", (128, 128), F32)
        scb128 = p.sb("scb128", (128, 128), BF16)
        ps_a = p.ps("ps_a", (128, 512), F32)
        ps_s = p.ps("ps_s0", (128, 128), F32)
        for bi in range(B):
            for ri in range(2):
                p.dma(xf[:, :], xld[bi, ri])
                p.copy(xfb[:, :], xf[:, :], q="vector")
                for g in range(8):
                    tr128(p, c, xfT[:, g, :], xfb[:, g * 128:(g + 1) * 128], 128, 128)
                for g2 in range(2):
                    for k in range(8):
                        p.mm(ps_a[:, :], xfT[:, k, :], qwT[:, k, g2 * 512:(g2 + 1) * 512], start=(k == 0), stop=(k == 7))
                    p.copy(q_sb[:, g2 * 512:(g2 + 1) * 512], ps_a[:, :], q="vector")
                for kv in range(4):
                    for k in range(8):
                        p.mm(ps_s[0:64, :], kwT[:, k, kv * 64:(kv + 1) * 64], xfT[:, k, :], start=(k == 0), stop=(k == 7))
                    p.copy(kT_hs[:, kv, :], ps_s[0:64, :], q="vector")
                for k in range(8):
                    p.mm(ps_a[:, 0:256], xfT[:, k, :], vwT[:, k, :], start=(k == 0), stop=(k == 7))
                p.copy(v_sb[:, :], ps_a[:, 0:256], q="vector")
                for h in range(16):
                    kv = h // 4
                    tr128(p, c, qTh[:, :], q_sb[:, h * 64:(h + 1) * 64], 128, 64)
                    p.mm(ps_s[:, :], qTh[:, :], kT_hs[:, kv, :], start=True, stop=True)
                    p.act(sc128[:, :], ps_s[:, :], AF.Identity, scale=0.125)
                    p.tt(sc128[:, :], sc128[:, :], bmask[:, :], "add")
                    p.red(c.c0[:, :], sc128[:, :], "max", negate=True)
                    p.act(sc128[:, :], sc128[:, :], AF.Exp, bias=c.c0[:, :])
                    p.red(c.c1[:, :], sc128[:, :], "add")
                    p.op("vector", lambda e: e.reciprocal(c.c1[:, :], c.c1[:, :]))
                    p.ts(sc128[:, :], sc128[:, :], c.c1[:, :], "mult")
                    p.copy(scb128[:, :], sc128[:, :], q="vector")
                    tr128(p, c, scb128[:, :], scb128[:, :], 128, 128)
                    p.mm(ps_s[0:64, :], v_sb[:, kv * 64:(kv + 1) * 64], scb128[:, :], start=True, stop=True)
                    goff = (h % 2) * 64
                    p.copy(aT[goff:goff + 64, h // 2, :], ps_s[0:64, :], q="vector")
                for g2 in range(2):
                    for k in range(8):
                        p.mm(ps_a[:, :], aT[:, k, :], owT[:, k, g2 * 512:(g2 + 1) * 512], start=(k == 0), stop=(k == 7))
                    p.copy(c.t1[:, g2 * 512:(g2 + 1) * 512], ps_a[:, :])
                p.tt(c.t1[:, :], c.t1[:, :], xf[:, :], "subtract")
                p.stt(c.t1[:, :], c.t1[:, :], acol[:, :], xf[:, :], "mult", "add")
                p.dma(alfd[bi, ri], c.t1[:, :])

    with p.scope():
        idftc = p.sb("idftc", (66, 64), F32)
        p.dma(idftc[:, :], Wd["c_idft"][:, :])
        g_exp = p.sb("g_exp", (128, 8, 1024), F32)
        Ah = p.sb("Ah", (128, 8, 1024), F32)
        Hx = p.sb("Hx", (128, 8, 1024), F32)
        yf = p.sb("yf", (66, 1024), F32)
        o64 = p.sb("o64", (64, 1024), F32)
        ps_i = p.ps("ps_i", (128, 512), F32)
        Ahd = p.dram(tag + "Ahd", (2, 1024, 1024), F32)
        for bi in range(B):
            for ch in range(NC):
                for (tt_, pp_, cnt_, off_) in _row_spans(ch * 29, 29):
                    p.dma(g_exp[pp_:pp_ + cnt_, tt_, :], gd[bi, ch:ch + 1, :].broadcast_to([cnt_, 1024]))
            for ri in range(2):
                for g in range(8):
                    p.dma(Hx[:, g, :], Htm[bi, ri, g * 128:(g + 1) * 128, :])
                base = 4 if ri == 0 else 37
                for ch in range(NC):
                    for (tt_, pp_, cnt_, off_) in _row_spans(ch * 29, 29):
                        p.dma(Ah[pp_:pp_ + cnt_, tt_, :], Yhat[bi, ch, base + off_:base + off_ + cnt_, :])
                for g in range(8):
                    p.tt(Hx[:, g, :], Hx[:, g, :], g_exp[:, g, :], "mult")
                    p.tt(Ah[:, g, :], Ah[:, g, :], Hx[:, g, :], "add")
                for g in range(8):
                    p.dma(Ahd[ri, g * 128:(g + 1) * 128, :], Ah[:, g, :])
            for ch in range(NC):
                p.dma(yf[0:4, :], alfd[bi, 0][ch * 4:(ch + 1) * 4, :])
                p.dma(yf[33:37, :], alfd[bi, 1][ch * 4:(ch + 1) * 4, :])
                p.dma(yf[4:33, :], Ahd[0, ch * 29:(ch + 1) * 29, :])
                p.dma(yf[37:66, :], Ahd[1, ch * 29:(ch + 1) * 29, :])
                for g2 in range(2):
                    p.mm(ps_i[0:64, :], idftc[:, :], yf[:, g2 * 512:(g2 + 1) * 512], start=True, stop=True)
                    p.copy(o64[:, g2 * 512:(g2 + 1) * 512], ps_i[0:64, :])
                p.dma(c.t0[0:64, :], xin[bi, ch * 64:(ch + 1) * 64, :])
                p.tt(o64[:, :], o64[:, :], c.t0[0:64, :], "add")
                p.dma(xout[bi, ch * 64:(ch + 1) * 64, :], o64[:, :])


SUBLAYERS = [
    ("a0", emit_l0attna), ("f0a", emit_l0ffn), ("b0", emit_l0attnb), ("f0b", emit_l0ffn),
    ("a1", emit_l1attna), ("f1a", emit_l1ffn), ("b1", emit_l1attnb), ("f1b", emit_l1ffn),
]
PREFIX = {"a0": "a0_", "f0a": "f0_", "b0": "b0_", "f0b": "f0_",
          "a1": "a1_", "f1a": "f1_", "b1": "b1_", "f1b": "f1_"}


def build_single(name, emit, Wnp, B, xin_np):
    p = P()
    with p.scope():
        c = Ctx(p, B)
        Wd = {}
        pref = PREFIX[name]
        for k, v in Wnp.items():
            if k.startswith(pref) or k.startswith("c_"):
                Wd[k] = p.inp(k, v)
        xin = p.inp("x_in", xin_np)
        xout = p.out("x_out", (B, SEQ, D), F32)
        ctx_init(p, c, Wd)
        emit(p, c, xin, xout, Wd, name)
    return p


NCORES = 8
BLOC = 4

CHAIN = [("a0", emit_l0attna), ("f0", emit_l0ffn), ("b0", emit_l0attnb), ("f0", emit_l0ffn),
         ("a1", emit_l1attna), ("f1", emit_l1ffn), ("b1", emit_l1attnb), ("f1", emit_l1ffn)]


def _build_fused(Wnp):
    p = P()
    with p.scope():
        c = Ctx(p, BLOC)
        Wd = {}
        wkeys = []
        for k in sorted(Wnp.keys()):
            Wd[k] = p.inp(k, Wnp[k])
            wkeys.append(k)
        xin = p.inp("x_in", np.zeros((BLOC, SEQ, D), np.float32))
        xout = p.out("x_out", (BLOC, SEQ, D), F32)
        ctx_init(p, c, Wd)
        xa = p.dram("xa_ping", (BLOC, SEQ, D), F32)
        xb = p.dram("xb_pong", (BLOC, SEQ, D), F32)
        cur = xin
        for i, (name, emit) in enumerate(CHAIN):
            nxt = xout if i == len(CHAIN) - 1 else (xa if i % 2 == 0 else xb)
            with p.scope():
                emit(p, c, cur, nxt, Wd, name + str(i))
            cur = nxt
    nc = p.finalize()
    return nc, wkeys, p


_cache = {}


def kernel(**inputs):
    Wnp = prep_weights(inputs)
    if "prog" not in _cache:
        _cache["prog"] = _build_fused(Wnp)
    nc, wkeys, _p = _cache["prog"]
    x = np.ascontiguousarray(np.asarray(inputs["L0_attn_a_hidden_states"], np.float32))
    assert x.shape[0] == NCORES * BLOC
    wmap = {k: Wnp[k] for k in wkeys}
    in_maps = [dict(wmap, x_in=np.ascontiguousarray(x[ci * BLOC:(ci + 1) * BLOC]))
               for ci in range(NCORES)]
    res = run_bass_kernel_spmd(nc, in_maps, core_ids=list(range(NCORES)))
    return np.concatenate([np.asarray(res.results[ci]["x_out"], np.float32) for ci in range(NCORES)], axis=0)
